# version 05
# baseline (speedup 1.0000x reference)
import contextlib

import numpy as np
import ml_dtypes

import concourse.bass as bass
import concourse.mybir as mybir
from concourse.bass_utils import run_bass_kernel_spmd

F32 = mybir.dt.float32
BF16 = mybir.dt.bfloat16
U8 = mybir.dt.uint8
AF = mybir.ActivationFunctionType
ALU = mybir.AluOpType
AX = mybir.AxisListType

D = 1024
H = 16
FF = 4096
RR = 64
II = 2816
NI = II // 128
EPS = 1e-6
C = 128
NJ = 8
MAGIC = float(1.5 * 2 ** 23)

V, Tn, Sc, Sy, Gp = "vector", "tensor", "scalar", "sync", "gpsimd"


SELF_WAIT = "self-wait"
SELF_WAIT_VS = "self-wait-vs"
SELF_WAIT_V = "self-wait-v"
SELF_WAIT_V1 = "self-wait-v-lag1"


class KB:
    def __init__(self, nc):
        self.nc = nc
        self.es = contextlib.ExitStack()
        self.sched = {k: [] for k in (V, Tn, Sc, Sy, Gp)}
        self.ctr = {k: 0 for k in (V, Tn, Sc, Sy, Gp, "out")}
        self.sem = {k: self.es.enter_context(nc.semaphore("sem_" + k))
                    for k in (V, Tn, Sc, Sy, Gp, "out")}
        self.bank_guard = {}

    def sb(self, name, shape, dt):
        return self.es.enter_context(self.nc.sbuf_tensor(name, shape, dt))

    def ps(self, name, shape, dt=F32):
        return self.es.enter_context(self.nc.psum_tensor(name, shape, dt))

    serial = False
    region = "init"

    def e(self, eng, fn, waits=(), sem=None, nsw=False):
        semk = sem if sem is not None else eng
        inc = 16 if semk in (Sy, Gp, "out") else 1
        my_sem = self.sem[semk]
        if self.serial is True or (
                self.serial and self.serial is not SELF_WAIT and self.region in self.serial):
            waits = list(waits) + [(k, self.ctr[k]) for k in (V, Tn, Sc, Sy, Gp, "out")]
        elif self.serial is SELF_WAIT and eng in (V, Tn, Sc):
            waits = list(waits) + [(eng, self.ctr[eng])]
        elif self.serial is SELF_WAIT_VS and eng in (V, Sc):
            waits = list(waits) + [(eng, self.ctr[eng])]
        elif self.serial is SELF_WAIT_V and eng == V:
            if not nsw:
                waits = list(waits) + [(eng, self.ctr[eng])]
        elif self.serial is SELF_WAIT_V1 and eng == V:
            waits = list(waits) + [(eng, self.ctr[eng] - 1)]
        wl = tuple((k, v) for (k, v) in waits if v > 0)

        def run(engine, fn=fn, wl=wl, my_sem=my_sem, inc=inc):
            for wk, wv in wl:
                engine.wait_ge(self.sem[wk], wv)
            fn(engine).then_inc(my_sem, inc)

        self.sched[eng].append(run)
        self.ctr[semk] += inc
        return self.ctr[semk]

    def wait(self, eng, waits):
        wl = tuple((k, v) for (k, v) in waits if v > 0)

        def run(engine, wl=wl):
            for wk, wv in wl:
                engine.wait_ge(self.sem[wk], wv)

        self.sched[eng].append(run)

    def now(self, *engs):
        return [(k, self.ctr[k]) for k in engs]


class _Cut(Exception):
    pass


def raise_cut(e):
    raise _Cut()


def build(T=2048, dbg=False, upto=None, serial=False):
    nc = bass.Bass("TRN2", target_bir_lowering=False, debug=False,
                   detect_race_conditions=False)
    kb = KB(nc)
    kb.serial = serial
    e = kb.e
    NT = T // 128
    NST = T // 512
    ST = 512
    NCHS = ST // C

    def din(name, shape, dt=F32):
        return nc.dram_tensor(name, shape, dt, kind="ExternalInput").ap()

    x_h = din("x", [T, D])
    w_q, w_k, w_v = din("q_w", [D, D]), din("k_w", [D, D]), din("v_w", [D, D])
    w_a, b_a = din("alpha_w", [D, D]), din("alpha_b", [D])
    w_b, b_b = din("beta_w", [D, H]), din("beta_b", [H])
    w_o = din("out_w", [D, D])
    w_cv, b_cv = din("conv_w", [D, 4]), din("conv_b", [D])
    nin_w = din("norm_in_w", [D])
    hn_w = din("head_norm_w", [H * 64])
    u1w, u1b = din("u1_w", [D, RR]), din("u1_b", [RR])
    u2w, u2b = din("u2_w", [RR, D]), din("u2_b", [D])
    ffn_w = din("ff_norm_w", [D])
    fw1, fw3, fw2 = din("ff_w1", [D, FF]), din("ff_w3", [D, FF]), din("ff_w2", [FF, D])
    clipx, clipd = din("clip_x", [1]), din("clip_down", [1])
    wg, wu, wd = din("w_gate", [II, D]), din("w_up", [II, D]), din("w_down", [D, II])
    c_id = din("c_ident", [128, 128], BF16)
    c_mu = din("c_masku", [128, 128], U8)
    c_mui = din("c_maskui", [128, 128], U8)
    c_ml = din("c_maskl", [128, 128], U8)
    c_E = din("c_E", [16, NJ * 128], BF16)
    c_F = din("c_F", [128, NJ * 16], BF16)
    c_ones = din("c_ones", [1, 128], BF16)
    c_onesf = din("c_onesf", [1, 128], F32)
    c_idf = din("c_identf", [128, 128], F32)

    out_h = nc.dram_tensor("out", [T, D], F32, kind="ExternalOutput").ap()
    _dbg_seen = {}

    def dump(name, ap_fn, shape, dt):
        if not dbg or name in _dbg_seen:
            return
        _dbg_seen[name] = True
        dh = nc.dram_tensor("dbg_" + name, shape, dt, kind="ExternalOutput").ap()
        kb.e(Sy, lambda y: y.dma_start(out=dh[:, :], in_=ap_fn()),
             waits=[(V, kb.ctr[V]), (Sc, kb.ctr[Sc]), (Tn, kb.ctr[Tn])], sem="out")
    cc1_i = nc.dram_tensor("cc1_in", [1, 16], F32, kind="Internal").ap()
    cc1_o = nc.dram_tensor("cc1_out", [1, 16], F32, kind="Internal").ap()
    cc2_i = nc.dram_tensor("cc2_in", [1, 16], F32, kind="Internal").ap()
    cc2_o = nc.dram_tensor("cc2_out", [1, 16], F32, kind="Internal").ap()
    xnew_h = nc.dram_tensor("xnew_scr", [T, D], BF16, kind="Internal").ap()

    P = [kb.ps(f"P{i}", [128, 512], F32) for i in range(7)]
    PT = kb.ps("PTr", [128, 1024], BF16)

    def bankw(i, extra=()):
        g = kb.bank_guard.get(i)
        w = list(extra)
        if g:
            w.append(g)
        return w

    def bank_done(i, eng):
        kb.bank_guard[i] = (eng, kb.ctr[eng])

    ident = kb.sb("s_ident", [128, 128], BF16)
    masku = kb.sb("s_masku", [128, 128], U8)
    maskui = kb.sb("s_maskui", [128, 128], U8)
    maskl = kb.sb("s_maskl", [128, 128], U8)
    Emat = kb.sb("s_E", [16, NJ * 128], BF16)
    Fmat = kb.sb("s_F", [128, NJ * 16], BF16)
    onesb = kb.sb("s_onesb", [1, 128], BF16)
    onesf = kb.sb("s_onesf", [1, 128], F32)
    identf = kb.sb("s_identf", [128, 128], F32)
    Sst = kb.sb("s_S", [128, NJ * 64], F32)
    carry = kb.sb("s_carry", [128, NJ], F32)
    m2s = kb.sb("s_m2s", [128, NT], F32)
    i2s = kb.sb("s_i2s", [128, NT], F32)
    gmax1 = kb.sb("s_gmax1", [128, 1], F32)
    gmax2 = kb.sb("s_gmax2", [128, NI], F32)
    clipb = kb.sb("s_clipb", [1, 2], F32)
    ccb = kb.sb("s_ccb", [1, 16], F32)
    stat = kb.sb("s_stat", [128, 16], F32)
    scr1 = kb.sb("s_scr1", [128, D], F32)
    scr2 = kb.sb("s_scr2", [128, D], F32)
    sW = contextlib.ExitStack()
    convw = sW.enter_context(nc.sbuf_tensor("s_convw", [128, 4 * NJ], F32))
    convb = sW.enter_context(nc.sbuf_tensor("s_convb", [128, NJ], F32))
    alphab = sW.enter_context(nc.sbuf_tensor("s_alphab", [128, NJ], F32))
    ninw = sW.enter_context(nc.sbuf_tensor("s_ninw", [128, NJ], F32))
    u1b_s = sW.enter_context(nc.sbuf_tensor("s_u1b", [64, 1], F32))
    betab = sW.enter_context(nc.sbuf_tensor("s_betab", [16, 1], F32))
    u2bb = sW.enter_context(nc.sbuf_tensor("s_u2bb", [128, D], BF16))
    ffnb = sW.enter_context(nc.sbuf_tensor("s_ffnb", [128, D], BF16))
    hnb = sW.enter_context(nc.sbuf_tensor("s_hnb", [128, D], BF16))
    u1w_s = sW.enter_context(nc.sbuf_tensor("s_u1w", [128, NJ * 64], BF16))
    u2w_s = sW.enter_context(nc.sbuf_tensor("s_u2w", [64, D], BF16))
    betaw = sW.enter_context(nc.sbuf_tensor("s_betaw", [128, NJ * 16], BF16))
    rowbuf = sW.enter_context(nc.sbuf_tensor("s_rowbuf", [1, 3 * D], BF16))

    ld = 0
    for a, b in [(ident, c_id), (masku, c_mu), (maskui, c_mui), (maskl, c_ml),
                 (Emat, c_E), (Fmat, c_F), (onesb, c_ones), (onesf, c_onesf),
                 (identf, c_idf)]:
        ld = e(Sy, lambda y, a=a, b=b: y.dma_start(out=a[:], in_=b[:, :]))
    ld = e(Sy, lambda y: y.dma_start(out=convw[:].rearrange("p (j k) -> p j k", j=NJ),
                                     in_=w_cv.rearrange("(j p) k -> p j k", p=128)))
    ld = e(Sy, lambda y: y.dma_start(out=convb[:], in_=b_cv.rearrange("(j p) -> p j", p=128), allow_slow_non_contiguous=True))
    ld = e(Sy, lambda y: y.dma_start(out=alphab[:], in_=b_a.rearrange("(j p) -> p j", p=128), allow_slow_non_contiguous=True))
    ld = e(Sy, lambda y: y.dma_start(out=ninw[:], in_=nin_w.rearrange("(j p) -> p j", p=128), allow_slow_non_contiguous=True))
    ld = e(Sy, lambda y: y.dma_start(out=u1b_s[:], in_=u1b.rearrange("(p o) -> p o", o=1)))
    ld = e(Sy, lambda y: y.dma_start(out=betab[:], in_=b_b.rearrange("(p o) -> p o", o=1)))
    ld = e(Sy, lambda y: y.dma_start(out=clipb[0:1, 0:1], in_=clipx.rearrange("(a b) -> a b", a=1)))
    ld = e(Sy, lambda y: y.dma_start(out=clipb[0:1, 1:2], in_=clipd.rearrange("(a b) -> a b", a=1)))
    ldg = e(Gp, lambda g: g.dma_start(out=u1w_s[:].rearrange("p (i r) -> p i r", i=NJ),
                                      in_=u1w.rearrange("(i p) r -> p i r", p=128)))
    ldg = e(Gp, lambda g: g.dma_start(out=u2w_s[:], in_=u2w[:, :]))
    ldg = e(Gp, lambda g: g.dma_start(out=betaw[:].rearrange("p (i h) -> p i h", i=NJ),
                                      in_=w_b.rearrange("(i p) h -> p i h", p=128)))
    ldg = e(Gp, lambda g: g.dma_start(out=rowbuf[0:1, 0:D], in_=u2b.rearrange("(a d) -> a d", a=1)))
    ldg = e(Gp, lambda g: g.dma_start(out=rowbuf[0:1, D:2 * D], in_=ffn_w.rearrange("(a d) -> a d", a=1)))
    ldg = e(Gp, lambda g: g.dma_start(out=rowbuf[0:1, 2 * D:3 * D], in_=hn_w.rearrange("(a d) -> a d", a=1)))
    LDC, LDG = ld, ldg

    for idx, dst in [(0, u2bb), (1, ffnb), (2, hnb)]:
        for hf in range(2):
            tm = e(Tn, lambda t, i=idx, hf=hf: t.matmul(
                P[0][:, 0:512], onesb[0:1, :],
                rowbuf[0:1, i * D + hf * 512: i * D + hf * 512 + 512],
                start=True, stop=True),
                waits=bankw(0, [(Sy, LDC), (Gp, LDG)]))
            e(V, lambda v, d=dst, hf=hf: v.tensor_copy(d[:, hf * 512:hf * 512 + 512], P[0][:, 0:512]),
              waits=[(Tn, tm)])
            bank_done(0, V)
    e(V, lambda v: v.tensor_mul(
        convw[:].rearrange("p (j k) -> p j k", j=NJ),
        convw[:].rearrange("p (j k) -> p j k", j=NJ),
        ninw[:].rearrange("p (j o) -> p j o", o=1).broadcast_to([128, NJ, 4])),
        waits=[(Sy, LDC)])
    e(V, lambda v: v.memset(Sst[:], 0.0))
    e(V, lambda v: v.memset(carry[:], 0.0))
    e(V, lambda v: v.memset(gmax1[:], 0.0))
    e(V, lambda v: v.memset(gmax2[:], 0.0))

    try:
     if upto == "consts":
        raise _Cut()
     with contextlib.ExitStack() as sA:
        def sb(n, s, d):
            return sA.enter_context(nc.sbuf_tensor(n, s, d))

        xld = sb("a_xld", [128, 2 * D], F32)
        xnbf = sb("a_xnbf", [128, D], BF16)
        lnx_t = sb("a_lnxT", [128, 4 * D], BF16)
        tail3 = sb("a_tail3", [128, NJ * 3], BF16)
        gate_t = sb("a_gate", [128, NCHS * D], BF16)
        azero = sb("a_azero", [128, ST], F32)
        yd_st = sb("a_ydst", [128, NJ * ST], BF16)
        wsA = sb("a_wsA", [128, 2 * NJ * 128], BF16)
        wsB = sb("a_wsB", [128, 2 * NJ * 128], BF16)
        wsC = sb("a_wsC", [128, 2 * NJ * 128], BF16)
        wrow = sb("a_wrow", [128, 2 * D], BF16)

        vz = e(V, lambda v: v.memset(tail3[:], 0.0))
        vz = e(V, lambda v: v.memset(azero[:], 0.0))

        xld_guard = [0, 0]

        def wslice(g, wsrc, buf, par, j):
            return g.dma_start(
                out=buf[:, par * NJ * 128:(par + 1) * NJ * 128].rearrange("p (i c) -> p i c", i=NJ),
                in_=wsrc.rearrange("(i p) o -> p i o", p=128)[:, :, j * 128:(j + 1) * 128])

        def _emit_st(st):
            s1 = contextlib.ExitStack()
            sb1 = lambda n, s, d, st=st: s1.enter_context(nc.sbuf_tensor(f"st{st}_" + n, s, d))
            hrms_t = sb1("hrmsT", [128, D], BF16)
            hrms_d = sb1("hrmsD", [128, NJ * (ST + 3)], BF16)
            lnx_d = sb1("lnxD", [128, NJ * ST], BF16)
            h_d = sb1("hD", [128, NJ * ST], BF16)
            qn = sb1("qn", [128, NJ * ST], BF16)
            kn = sb1("kn", [128, NJ * ST], BF16)
            kbet = sb1("kbeta", [128, NJ * ST], BF16)
            lnal = sb1("lnal", [128, 2 * ST], F32)
            cst = sb1("cst", [128, NJ * (ST + 1)], F32)
            vpr = sb1("vpr", [128, NCHS * D], BF16)
            silu1 = sb1("silu1", [64, ST], BF16)
            beta_d = sb1("betaD", [16, ST], BF16)
            beta_t = sb1("betaT", [128, NCHS * 16], BF16)
            sqs = sb1("sqs", [128, 2 * ST], BF16)
            normb = sb1("normb", [16, ST], F32)
            ecb = sb1("ecb", [128, NJ * (C + 1)], F32)
            emcb = sb1("emcb", [128, NJ * (C + 1)], F32)
            mpos = sb1("mpos", [128, NJ], F32)
            mneg = sb1("mneg", [128, NJ], F32)
            bCs = sb1("bCs", [128, NJ], F32)
            kapb = sb1("kapb", [128, NJ * C], BF16)
            kprc = sb1("kprc", [128, NJ * C], BF16)
            kpad = sb1("kpad", [128, NJ * C], BF16)
            qcc = sb1("qcc", [128, NJ * C], BF16)
            kccb = sb1("kcc", [128, NJ * C], BF16)
            kcct = sb1("kcct", [128, NJ * C], BF16)
            s0p = sb1("s0p", [128, NJ * 64], BF16)
            powN = sb1("powN", [128, 2 * 128], BF16)
            powT = sb1("powT", [128, 2 * 128], BF16)
            pl1t = sb1("pl1t", [128, 2 * 128], BF16)
            lqts = sb1("lqt", [128, 2 * 128], BF16)
            xb = sb1("xb", [128, 2 * 64], BF16)
            ystg = sb1("ystg", [128, D], F32)
            ybf = sb1("ybf", [128, D], BF16)
            vz = e(V, lambda v: v.memset(pl1t[:], 0.0), waits=[(Tn, kb.ctr[Tn])])
            vz = e(V, lambda v: v.memset(kpad[:], 0.0))
            vz = e(V, lambda v: v.memset(lqts[:], 0.0))
            vz = e(V, lambda v: v.memset(powN[:], 0.0))
            vz = e(V, lambda v: v.memset(powT[:], 0.0))
            kb.region = "ln"
            for ti in range(4):
                gt = st * 4 + ti
                par = gt % 2
                xd = e(Sy, lambda y, g=gt, p=par: y.dma_start(
                    out=xld[:, p * D:(p + 1) * D], in_=x_h[g * 128:(g + 1) * 128, :]),
                    waits=[(V, xld_guard[par])])
                xsl = lambda p=par: xld[:, p * D:(p + 1) * D]
                vw = e(V, lambda v, p=par: v.reduce_sum(stat[:, 0:1], xld[:, p * D:(p + 1) * D], AX.X),
                       waits=[(Sy, xd), (Sc, kb.ctr[Sc]), (V, kb.ctr[V])])
                vw = e(V, lambda v: v.tensor_scalar_mul(stat[:, 1:2], stat[:, 0:1], -1.0 / D))
                sw = e(Sc, lambda s, p=par: s.activation(scr1[:], xld[:, p * D:(p + 1) * D], AF.Square,
                                                         bias=stat[:, 1:2], scale=1.0,
                                                         accum_out=stat[:, 2:3]),
                       waits=[(V, vw)])
                vw = e(V, lambda v: v.tensor_scalar(stat[:, 3:4], stat[:, 2:3], 1.0 / D, 1e-5,
                                                    ALU.mult, ALU.add), waits=[(Sc, sw)])
                vw = e(V, lambda v: v.reciprocal(stat[:, 4:5], stat[:, 3:4]))
                sw = e(Sc, lambda s: s.activation(stat[:, 5:6], stat[:, 4:5], AF.Sqrt), waits=[(V, vw)])
                vw = e(V, lambda v: v.tensor_scalar(stat[:, 6:7], stat[:, 2:3], 1.0 / D, 0.0,
                                                    ALU.mult, ALU.add), waits=[(Sc, sw)])
                vw = e(V, lambda v: v.tensor_mul(stat[:, 7:8], stat[:, 6:7], stat[:, 4:5]))
                vw = e(V, lambda v: v.tensor_scalar(stat[:, 7:8], stat[:, 7:8], 1.0, EPS, ALU.mult, ALU.add))
                vw = e(V, lambda v: v.reciprocal(stat[:, 8:9], stat[:, 7:8]))
                sw = e(Sc, lambda s: s.activation(stat[:, 9:10], stat[:, 8:9], AF.Sqrt), waits=[(V, vw)])
                vw = e(V, lambda v: v.tensor_mul(stat[:, 10:11], stat[:, 9:10], stat[:, 5:6]),
                       waits=[(Sc, sw)])
                vw = e(V, lambda v, ti=ti, p=par: v.tensor_scalar(
                    lnx_t[:, ti * D:(ti + 1) * D], xld[:, p * D:(p + 1) * D],
                    stat[:, 1:2], stat[:, 5:6], ALU.add, ALU.mult), waits=[(V, vw)])
                vw = e(V, lambda v, p=par: v.tensor_scalar(
                    hrms_t[:, 0:D], xld[:, p * D:(p + 1) * D],
                    stat[:, 1:2], stat[:, 10:11], ALU.add, ALU.mult))
                xld_guard[par] = vw
                for jg in range(2):
                    tws = 0
                    for jj in range(4):
                        j = jg * 4 + jj
                        jw = bankw(7, [(V, vw)]) if jj == 0 else ()
                        tws = e(Tn, lambda t, j=j, ti=ti, q=jj: t.transpose(
                            PT[:, q * 256:q * 256 + 128],
                            lnx_t[:, ti * D + j * 128:ti * D + j * 128 + 128], ident[:]),
                            waits=jw)
                        tws = e(Tn, lambda t, j=j, q=jj: t.transpose(
                            PT[:, q * 256 + 128:q * 256 + 256],
                            hrms_t[:, j * 128:j * 128 + 128], ident[:]))
                    vv = e(V, lambda v, jg=jg, ti=ti: v.tensor_copy(
                        lnx_d[:].rearrange("p (j t) -> p j t", j=NJ)[
                            :, jg * 4:jg * 4 + 4, ti * 128:ti * 128 + 128],
                        PT[:, 0:1024].rearrange("p (q x) -> p q x", q=4)[:, :, 0:128]),
                        waits=[(Tn, tws)], nsw=True)
                    vv = e(V, lambda v, jg=jg, ti=ti: v.tensor_copy(
                        hrms_d[:].rearrange("p (j c) -> p j c", j=NJ)[
                            :, jg * 4:jg * 4 + 4, 3 + ti * 128:3 + ti * 128 + 128],
                        PT[:, 0:1024].rearrange("p (q x) -> p q x", q=4)[:, :, 128:256]),
                        nsw=True)
                    bank_done(7, V)
            vv = e(V, lambda v: v.tensor_copy(
                hrms_d[:].rearrange("p (j c) -> p j c", j=NJ)[:, :, 0:3],
                tail3[:].rearrange("p (j c) -> p j c", j=NJ)))
            for j in range(NJ):
                base = j * (ST + 3)
                vv = e(V, lambda v, b=base, j=j: v.tensor_scalar(
                    scr2[:, 0:ST], hrms_d[:, b:b + ST], convw[:, 4 * j:4 * j + 1], None, ALU.mult),
                    waits=[(Sc, kb.ctr[Sc])])
                for kk in range(1, 4):
                    vv = e(V, lambda v, b=base, j=j, kk=kk: v.scalar_tensor_tensor(
                        scr2[:, 0:ST], hrms_d[:, b + kk:b + kk + ST],
                        convw[:, 4 * j + kk:4 * j + kk + 1], scr2[:, 0:ST], ALU.mult, ALU.add))
                sch = e(Sc, lambda s, j=j: s.activation(h_d[:, j * ST:(j + 1) * ST], scr2[:, 0:ST],
                                                        AF.Silu, bias=convb[:, j:j + 1], scale=1.0),
                        waits=[(V, vv)])
            SC_H = kb.ctr[Sc]
            vv = e(V, lambda v: v.tensor_copy(
                tail3[:].rearrange("p (j c) -> p j c", j=NJ),
                hrms_d[:].rearrange("p (j c) -> p j c", j=NJ)[:, :, ST:ST + 3]),
                waits=[(Sc, SC_H)])

            if upto == "a1":
                raise _Cut()
            kb.region = "proj"
            gmk = {}
            g0 = e(Gp, lambda g: wslice(g, w_q, wsA, 0, 0), waits=[(Tn, kb.ctr[Tn])])
            g0 = e(Gp, lambda g: wslice(g, w_k, wsB, 0, 0))
            g0 = e(Gp, lambda g: wslice(g, w_a, wsC, 0, 0))
            gmk[0] = g0
            last_mm_par = [0, 0]
            for j in range(NJ):
                par = j % 2
                if j + 1 < NJ:
                    gn = e(Gp, lambda g, j=j: wslice(g, w_q, wsA, (j + 1) % 2, j + 1),
                           waits=[(Tn, last_mm_par[(j + 1) % 2])])
                    gn = e(Gp, lambda g, j=j: wslice(g, w_k, wsB, (j + 1) % 2, j + 1))
                    gn = e(Gp, lambda g, j=j: wslice(g, w_a, wsC, (j + 1) % 2, j + 1))
                    gmk[j + 1] = gn
                tq = tk = ta = 0
                for i in range(NJ):
                    tq = e(Tn, lambda t, i=i, p=par: t.matmul(
                        P[0][:, 0:512], wsA[:, p * 1024 + i * 128:p * 1024 + i * 128 + 128],
                        h_d[:, i * ST:i * ST + 512], start=(i == 0), stop=(i == NJ - 1)),
                        waits=(bankw(0, [(Gp, gmk[j]), (Sc, SC_H)]) if i == 0 else ()))
                vq = e(V, lambda v, j=j: v.tensor_copy(qn[:, j * ST:(j + 1) * ST], P[0][:, 0:512]),
                       waits=[(Tn, tq)], nsw=True)
                bank_done(0, V)
                for i in range(NJ):
                    tk = e(Tn, lambda t, i=i, p=par: t.matmul(
                        P[1][:, 0:512], wsB[:, p * 1024 + i * 128:p * 1024 + i * 128 + 128],
                        h_d[:, i * ST:i * ST + 512], start=(i == 0), stop=(i == NJ - 1)),
                        waits=(bankw(1) if i == 0 else ()))
                vk = e(V, lambda v, j=j: v.tensor_copy(kn[:, j * ST:(j + 1) * ST], P[1][:, 0:512]),
                       waits=[(Tn, tk)], nsw=True)
                bank_done(1, V)
                for i in range(NJ):
                    ta = e(Tn, lambda t, i=i, p=par: t.matmul(
                        P[2][:, 0:512], wsC[:, p * 1024 + i * 128:p * 1024 + i * 128 + 128],
                        h_d[:, i * ST:i * ST + 512], start=(i == 0), stop=(i == NJ - 1)),
                        waits=(bankw(2) if i == 0 else ()))
                lp = j % 2
                sa1 = e(Sc, lambda s, j=j, lp=lp: s.activation(
                    lnal[:, lp * ST:lp * ST + ST], P[2][:, 0:512], AF.Sigmoid,
                    bias=alphab[:, j:j + 1], scale=1.0),
                    waits=[(Tn, ta), (V, kb.ctr[V])])
                sa1 = e(Sc, lambda s, lp=lp: s.activation(
                    lnal[:, lp * ST:lp * ST + ST], lnal[:, lp * ST:lp * ST + ST], AF.Ln))
                bank_done(2, Sc)
                cb = j * (ST + 1)
                vs_ = e(V, lambda v, cb=cb, j=j: v.tensor_copy(cst[:, cb:cb + 1], carry[:, j:j + 1]),
                        waits=[(Sc, sa1), (V, kb.ctr[V])])
                vs_ = e(V, lambda v, cb=cb, lp=lp: v.tensor_tensor_scan(
                    cst[:, cb + 1:cb + 1 + ST], lnal[:, lp * ST:lp * ST + ST],
                    azero[:, 0:ST], cst[:, cb:cb + 1], ALU.add, ALU.add),
                    waits=[(V, vs_)])
                vs_ = e(V, lambda v, cb=cb, j=j: v.tensor_copy(carry[:, j:j + 1],
                                                               cst[:, cb + ST:cb + ST + 1]),
                        waits=[(V, vs_)])
                last_mm_par[par] = ta
            for buf, tens in [(0, qn), (1, kn)]:
                tpv = 0
                for j in range(NJ):
                    vsq = e(V, lambda v, j=j, b=buf, tens=tens: v.tensor_mul(
                        sqs[:, b * ST:(b + 1) * ST], tens[:, j * ST:(j + 1) * ST],
                        tens[:, j * ST:(j + 1) * ST]), waits=[(Tn, tpv), (V, kb.ctr[V])],
                        nsw=True)
                    tpv = e(Tn, lambda t, j=j, b=buf: t.matmul(
                        P[5][0:16, 0:512], Fmat[:, j * 16:(j + 1) * 16], sqs[:, b * ST:(b + 1) * ST],
                        start=(j == 0), stop=(j == NJ - 1)),
                        waits=(bankw(5, [(V, vsq)]) if j == 0 else [(V, vsq)]))
                sn = e(Sc, lambda s: s.activation(normb[0:16, 0:512], P[5][0:16, 0:512], AF.Sqrt),
                    waits=[(Tn, tpv), (V, kb.ctr[V])])
                bank_done(5, Sc)
                vn = e(V, lambda v: v.tensor_scalar(normb[0:16, 0:512], normb[0:16, 0:512], 1.0, EPS,
                                                    ALU.mult, ALU.add), waits=[(Sc, sn)])
                vn = e(V, lambda v: v.reciprocal(normb[0:16, 0:512], normb[0:16, 0:512]))
                vn = e(V, lambda v, b=buf: v.tensor_copy(sqs[0:16, b * ST:b * ST + 512],
                                                         normb[0:16, 0:512]), waits=[(V, vn)])
                for j in range(NJ):
                    te2 = e(Tn, lambda t, j=j, b=buf: t.matmul(
                        P[4][:, 0:512], Emat[:, j * 128:(j + 1) * 128],
                        sqs[0:16, b * ST:b * ST + 512], start=True, stop=True),
                        waits=bankw(4, [(V, vn)]))
                    e(V, lambda v, j=j, tens=tens: v.tensor_mul(tens[:, j * ST:(j + 1) * ST],
                                                     tens[:, j * ST:(j + 1) * ST], P[4][:, 0:512]),
                      waits=[(Tn, te2)], nsw=True)
                    bank_done(4, V)
            VQK = kb.ctr[V]
            if st == 0:
                dump("qinv", lambda: sqs[0:16, 0:512], [16, 512], BF16)
                dump("zq_post", lambda: qn[:, 0:NJ * ST], [128, NJ * ST], BF16)
            if upto == "qk":
                raise _Cut()
            tb = 0
            for i in range(NJ):
                tb = e(Tn, lambda t, i=i: t.matmul(
                    P[5][0:16, 0:512], betaw[:, i * 16:(i + 1) * 16], h_d[:, i * ST:i * ST + 512],
                    start=(i == 0), stop=(i == NJ - 1)),
                    waits=(bankw(5, [(Gp, LDG)]) if i == 0 else ()))
            sbeta = e(Sc, lambda s: s.activation(beta_d[:, 0:ST], P[5][0:16, 0:512], AF.Sigmoid,
                                               bias=betab[0:16, 0:1], scale=1.0), waits=[(Tn, tb)])
            bank_done(5, Sc)
            for ccw in range(NCHS):
                tb2 = e(Tn, lambda t, cc=ccw: t.transpose(
                    PT[:, cc * 16:cc * 16 + 16],
                    beta_d[0:16, cc * 128:(cc + 1) * 128], ident[0:16, 0:16]),
                    waits=(bankw(7, [(Sc, sbeta)]) if ccw == 0 else ()))
            vbt = e(V, lambda v: v.tensor_copy(beta_t[:, 0:NCHS * 16], PT[:, 0:NCHS * 16]),
                    waits=[(Tn, kb.ctr[Tn])])
            bank_done(7, V)
            for j in range(NJ):
                te2 = e(Tn, lambda t, j=j: t.matmul(
                    P[4][:, 0:512], Emat[:, j * 128:(j + 1) * 128], beta_d[0:16, 0:512],
                    start=True, stop=True), waits=bankw(4, [(Sc, sbeta), (V, VQK)]))
                e(V, lambda v, j=j: v.tensor_mul(kbet[:, j * ST:(j + 1) * ST],
                                                 kn[:, j * ST:(j + 1) * ST], P[4][:, 0:512]),
                  waits=[(Tn, te2)])
                bank_done(4, V)
            wv_guard = [0, 0]
            for hf in range(2):
                gv = 0
                tvcc = [0, 0, 0, 0]
                for i in range(NJ):
                    par = i % 2
                    gv = e(Gp, lambda g, i=i, h=hf, p=par: g.dma_start(
                        out=wrow[:, p * D:p * D + 512],
                        in_=w_v[i * 128:(i + 1) * 128, h * 512:h * 512 + 512]),
                        waits=[(Tn, wv_guard[par])] + (
                            [(Tn, last_mm_par[0]), (Tn, last_mm_par[1]), (V, VQK)] if i == 0 and hf == 0 else []))
                    for ccv in range(NCHS):
                        w0 = []
                        if i == 0:
                            w0 = bankw(ccv, [(Gp, gv)])
                        tvcc[ccv] = e(Tn, lambda t, i=i, cc=ccv, p=par: t.matmul(
                            P[cc][:, 0:512], h_d[:, i * ST + cc * 128:i * ST + cc * 128 + 128],
                            wrow[:, p * D:p * D + 512],
                            start=(i == 0), stop=(i == NJ - 1)),
                            waits=(w0 + ([(Gp, gv)] if i > 0 and ccv == 0 else [])))
                    wv_guard[par] = tvcc[NCHS - 1]
                for ccv in range(NCHS):
                    e(V, lambda v, cc=ccv, h=hf: v.tensor_mul(
                        vpr[:, cc * D + h * 512:cc * D + h * 512 + 512].rearrange(
                            "p (hh vv) -> p hh vv", vv=64),
                        P[cc][:, 0:512].rearrange("p (hh vv) -> p hh vv", vv=64),
                        beta_t[:, cc * 16 + h * 8:cc * 16 + h * 8 + 8].rearrange(
                            "p (hh o) -> p hh o", o=1).broadcast_to([128, 8, 64])),
                      waits=[(Tn, tvcc[ccv]), (V, vbt)])
                    bank_done(ccv, V)
            tu = 0
            for i in range(NJ):
                tu = e(Tn, lambda t, i=i: t.matmul(
                    P[2][0:64, 0:512], u1w_s[:, i * 64:(i + 1) * 64], lnx_d[:, i * ST:i * ST + 512],
                    start=(i == 0), stop=(i == NJ - 1)),
                    waits=(bankw(2, [(Gp, LDG)]) if i == 0 else ()))
            su = e(Sc, lambda s: s.activation(silu1[:, 0:ST], P[2][0:64, 0:512], AF.Silu,
                                              bias=u1b_s[0:64, 0:1], scale=1.0), waits=[(Tn, tu)])
            bank_done(2, Sc)
            for ccx in range(NCHS):
                for hf in range(2):
                    tg = e(Tn, lambda t, cc=ccx, h=hf: t.matmul(
                        P[2][:, 0:512], silu1[0:64, cc * 128:(cc + 1) * 128],
                        u2w_s[0:64, h * 512:h * 512 + 512], start=True, stop=True),
                        waits=bankw(2, [(Sc, su)]))
                    vg = e(V, lambda v, h=hf: v.tensor_add(scr1[:, 0:512], P[2][:, 0:512],
                                                           u2bb[:, h * 512:h * 512 + 512]),
                           waits=[(Tn, tg), (Sc, kb.ctr[Sc])])
                    bank_done(2, V)
                    e(Sc, lambda s, cc=ccx, h=hf: s.activation(
                        gate_t[:, cc * D + h * 512:cc * D + h * 512 + 512], scr1[:, 0:512],
                        AF.Sigmoid), waits=[(V, vg)])
            SC_GATE = kb.ctr[Sc]

            if st == 0:
                dump("lnx_t", lambda: lnx_t[:, 0:4 * D], [128, 4 * D], BF16)
                dump("h_d", lambda: h_d[:], [128, NJ * ST], BF16)
                dump("qn", lambda: qn[:], [128, NJ * ST], BF16)
                dump("kn", lambda: kn[:], [128, NJ * ST], BF16)
                dump("kbet", lambda: kbet[:], [128, NJ * ST], BF16)
                dump("cst", lambda: cst[:], [128, NJ * (ST + 1)], F32)
                dump("vpr", lambda: vpr[:], [128, NCHS * D], BF16)
                dump("gate_t", lambda: gate_t[:], [128, NCHS * D], BF16)
            if upto == "a2":
                raise _Cut()
            kb.region = "scan"
            for ccx in range(NCHS):
                vm = 0
                for j in range(NJ):
                    cb = j * (ST + 1) + ccx * 128
                    vm = e(V, lambda v, j=j, cb=cb: v.tensor_scalar(
                        mpos[:, j:j + 1], cst[:, cb:cb + 1], cst[:, cb + 128:cb + 129],
                        0.5, ALU.add, ALU.mult),
                        waits=([(V, kb.ctr[V]), (Sc, kb.ctr[Sc]), (Tn, kb.ctr[Tn]),
                                ("out", kb.ctr["out"])] if j == 0 else ()))
                    vm = e(V, lambda v, j=j, cb=cb: v.tensor_scalar(
                        mneg[:, j:j + 1], cst[:, cb:cb + 1], cst[:, cb + 128:cb + 129],
                        -0.5, ALU.add, ALU.mult))
                se2 = 0
                for j in range(NJ):
                    cb = j * (ST + 1) + ccx * 128
                    eb = j * (C + 1)
                    se2 = e(Sc, lambda s, cb=cb, eb=eb, j=j: s.activation(
                        ecb[:, eb:eb + C + 1], cst[:, cb:cb + C + 1], AF.Exp,
                        bias=mneg[:, j:j + 1], scale=1.0),
                        waits=([(V, vm)] if j == 0 else ()))
                    se2 = e(Sc, lambda s, cb=cb, eb=eb, j=j: s.activation(
                        emcb[:, eb:eb + C + 1], cst[:, cb:cb + C + 1], AF.Exp,
                        bias=mpos[:, j:j + 1], scale=-1.0))
                kn3 = kn[:].rearrange("p (j t) -> p j t", j=NJ)[:, :, ccx * 128:(ccx + 1) * 128]
                kb3 = kbet[:].rearrange("p (j t) -> p j t", j=NJ)[:, :, ccx * 128:(ccx + 1) * 128]
                qn3 = qn[:].rearrange("p (j t) -> p j t", j=NJ)[:, :, ccx * 128:(ccx + 1) * 128]
                ec3 = ecb[:].rearrange("p (j t) -> p j t", j=NJ)[:, :, 1:C + 1]
                em3 = emcb[:].rearrange("p (j t) -> p j t", j=NJ)[:, :, 1:C + 1]
                vx = e(V, lambda v, a=kn3, b=em3: v.tensor_mul(
                    kapb[:].rearrange("p (j t) -> p j t", j=NJ), a, b),
                    waits=[(Sc, se2), (V, kb.ctr[V]), (Tn, kb.ctr[Tn])])
                vx = e(V, lambda v, a=kb3, b=ec3: v.tensor_mul(
                    kprc[:].rearrange("p (j t) -> p j t", j=NJ), a, b), nsw=True)
                vx = e(V, lambda v, a=qn3, b=ec3: v.tensor_mul(
                    qcc[:].rearrange("p (j t) -> p j t", j=NJ), a, b), nsw=True)
                vx = e(V, lambda v: v.tensor_copy(
                    kpad[:].rearrange("p (j t) -> p j t", j=NJ)[:, :, 64:128],
                    kprc[:].rearrange("p (j t) -> p j t", j=NJ)[:, :, 64:128]), nsw=True)
                for j in range(NJ):
                    eb = j * (C + 1)
                    vx = e(V, lambda v, j=j, eb=eb: v.tensor_scalar(
                        kccb[:, j * C:(j + 1) * C], kapb[:, j * C:(j + 1) * C],
                        ecb[:, eb + C:eb + C + 1], None, ALU.mult))
                    vx = e(V, lambda v, j=j, eb=eb: v.tensor_mul(
                        bCs[:, j:j + 1], ecb[:, eb + C:eb + C + 1], emcb[:, eb:eb + 1]))
                    vx = e(V, lambda v, j=j, eb=eb: v.tensor_scalar(
                        s0p[:, j * 64:(j + 1) * 64], Sst[:, j * 64:(j + 1) * 64],
                        emcb[:, eb:eb + 1], None, ALU.mult))
                VPREP = kb.ctr[V]
                if st == 0 and ccx == 0:
                    dump("ecb", lambda: ecb[:], [128, NJ * (C + 1)], F32)
                    dump("emcb", lambda: emcb[:], [128, NJ * (C + 1)], F32)
                    dump("kapb", lambda: kapb[:], [128, NJ * C], BF16)
                    dump("kprc", lambda: kprc[:], [128, NJ * C], BF16)
                    dump("qcc", lambda: qcc[:], [128, NJ * C], BF16)
                    dump("kccb", lambda: kccb[:], [128, NJ * C], BF16)
                    dump("s0p", lambda: s0p[:], [128, NJ * 64], BF16)
                    dump("bCs", lambda: bCs[:], [128, NJ], F32)
                tkt = 0
                for j in range(NJ):
                    tkt = e(Tn, lambda t, j=j: t.transpose(
                        PT[:, j * 128:(j + 1) * 128], kccb[:, j * C:(j + 1) * C], ident[:]),
                        waits=(bankw(7, [(V, VPREP)]) if j == 0 else ()))
                vkt = e(V, lambda v: v.tensor_copy(kcct[:, 0:512], PT[:, 0:512]), waits=[(Tn, tkt)],
                        nsw=True)
                vkt = e(V, lambda v: v.tensor_copy(kcct[:, 512:1024], PT[:, 512:1024]), nsw=True)
                bank_done(7, V)
                evac_par = [0, 0]
                state_ev = 0
                for hh in range(H):
                    j, hp, par = hh // 2, hh % 2, hh % 2
                    rs = slice(64 * hp, 64 * hp + 64)
                    kap_ = lambda lo, sz, j=j, rs=rs: kapb[rs, j * C + lo:j * C + lo + sz]
                    kpr_ = lambda lo, sz, j=j, rs=rs: kprc[rs, j * C + lo:j * C + lo + sz]
                    qc_ = lambda lo, sz, j=j, rs=rs: qcc[rs, j * C + lo:j * C + lo + sz]
                    pA = P[0 + par]
                    pB = P[2 + par]
                    pR = P[4 + par]
                    pS = P[6]
                    tm = e(Tn, lambda t, f=kap_, g2=kpr_, pA=pA: t.matmul(
                        pA[0:64, 0:64], f(0, 64), g2(0, 64), start=True, stop=True),
                        waits=[(V, VPREP), (V, evac_par[par]), (V, vkt),
                               ("out", kb.ctr["out"])])
                    tm = e(Tn, lambda t, f=kap_, g2=kpr_, pA=pA: t.matmul(
                        pA[:, 64:128], f(0, 128), g2(64, 64), start=True, stop=True))
                    tm = e(Tn, lambda t, f=kap_, g2=kpr_, pB=pB: t.matmul(
                        pB[:, 0:64], g2(0, 128), f(0, 64), start=True, stop=True))
                    tmf = e(Tn, lambda t, f=kap_, j=j, rs=rs, pB=pB: t.matmul(
                        pB[:, 64:128], kpad[rs, j * C:(j + 1) * C], f(64, 64),
                        start=True, stop=True))
                    vv1 = e(V, lambda v, pA=pA, par=par: v.copy_predicated(
                        pl1t[:, par * 128:(par + 1) * 128], masku[:], pA[:, 0:128]),
                        waits=[(Tn, tmf)])
                    vv1 = e(V, lambda v, pB=pB, par=par: v.copy_predicated(
                        powN[:, par * 128:(par + 1) * 128], maskl[:], pB[:, 0:128]))
                    tm = e(Tn, lambda t, f=kap_, q2=qc_, pA=pA: t.matmul(
                        pA[0:64, 128:192], f(0, 64), q2(0, 64), start=True, stop=True),
                        waits=[(V, vv1)])
                    tmq = e(Tn, lambda t, f=kap_, q2=qc_, pA=pA: t.matmul(
                        pA[:, 192:256], f(0, 128), q2(64, 64), start=True, stop=True))
                    vv2 = e(V, lambda v, pA=pA, par=par: v.copy_predicated(
                        lqts[:, par * 128:(par + 1) * 128], maskui[:], pA[:, 128:256]),
                        waits=[(Tn, tmq)])
                    tm = e(Tn, lambda t, g2=kpr_, j=j, hp=hp, pR=pR: t.matmul(
                        pR[:, 0:64], g2(0, 128), s0p[64 * hp:64 * hp + 64, j * 64:j * 64 + 64],
                        start=True, stop=True))
                    vv3 = e(V, lambda v, cc=ccx, hh=hh, par=par, pR=pR: v.tensor_sub(
                        xb[:, par * 64:(par + 1) * 64],
                        vpr[:, cc * D + hh * 64:cc * D + hh * 64 + 64], pR[:, 0:64]),
                        waits=[(Tn, tm)])
                    if st == 0 and ccx == 0 and hh == 0:
                        dump("plt0", lambda: pl1t[:, 0:128], [128, 128], BF16)
                        dump("pln0", lambda: powN[:, 0:128], [128, 128], BF16)
                        dump("lqt0", lambda: lqts[:, 0:128], [128, 128], BF16)
                        dump("rp0", lambda: xb[:, 0:64], [128, 64], BF16)
                    tm = e(Tn, lambda t, par=par, pR=pR: t.matmul(
                        pR[:, 64:128], pl1t[:, par * 128:(par + 1) * 128],
                        xb[:, par * 64:(par + 1) * 64], start=True, stop=True), waits=[(V, vv3)])
                    vap = e(V, lambda v, par=par, pR=pR: v.tensor_sub(
                        xb[:, par * 64:(par + 1) * 64], xb[:, par * 64:(par + 1) * 64],
                        pR[:, 64:128]), waits=[(Tn, tm)])
                    srcN, srcT = powN, pl1t
                    scop = kb.ctr[Sc]
                    lev = 2
                    while lev <= 64:
                        if lev < 64:
                            tmN = e(Tn, lambda t, a=srcT, b=srcN, par=par, pA=pA: t.matmul(
                                pA[:, 256:384], a[:, par * 128:(par + 1) * 128],
                                b[:, par * 128:(par + 1) * 128], start=True, stop=True),
                                waits=[(Sc, scop), (V, vap)])
                        tmT = e(Tn, lambda t, a=srcN, b=srcT, par=par, pA=pA: t.matmul(
                            pA[:, 384:512], a[:, par * 128:(par + 1) * 128],
                            b[:, par * 128:(par + 1) * 128], start=True, stop=True),
                            waits=([(Sc, scop), (V, vap)] if lev == 64 else ()))
                        if lev < 64:
                            scop = e(Sc, lambda s, par=par, pA=pA: s.copy(
                                powN[:, par * 128:(par + 1) * 128], pA[:, 256:384]),
                                waits=[(Tn, tmT)])
                        scop = e(Sc, lambda s, par=par, pA=pA: s.copy(
                            powT[:, par * 128:(par + 1) * 128], pA[:, 384:512]),
                            waits=([(Tn, tmT)] if lev == 64 else ()))
                        tm = e(Tn, lambda t, par=par, pR=pR: t.matmul(
                            pR[:, 64:128], powT[:, par * 128:(par + 1) * 128],
                            xb[:, par * 64:(par + 1) * 64], start=True, stop=True),
                            waits=[(Sc, scop)])
                        vap = e(V, lambda v, par=par, pR=pR: v.tensor_add(
                            xb[:, par * 64:(par + 1) * 64], xb[:, par * 64:(par + 1) * 64],
                            pR[:, 64:128]), waits=[(Tn, tm)])
                        srcN, srcT = powN, powT
                        lev *= 2
                    if st == 0 and ccx == 0 and hh == 0:
                        dump("u0", lambda: xb[:, 0:64], [128, 64], BF16)
                    tm = e(Tn, lambda t, q2=qc_, j=j, hp=hp, pR=pR: t.matmul(
                        pR[:, 128:192], q2(0, 128),
                        s0p[64 * hp:64 * hp + 64, j * 64:j * 64 + 64], start=True, stop=False),
                        waits=[(V, vap)])
                    tm = e(Tn, lambda t, par=par, pR=pR: t.matmul(
                        pR[:, 128:192], lqts[:, par * 128:(par + 1) * 128],
                        xb[:, par * 64:(par + 1) * 64], start=False, stop=True))
                    sq1 = e(Sc, lambda s, hh=hh, pR=pR: s.activation(
                        scr1[:, (hh % 8) * 64:(hh % 8) * 64 + 64], pR[:, 128:192], AF.Square,
                        accum_out=stat[:, 11:12]), waits=[(Tn, tm), (V, kb.ctr[V])])
                    vr = e(V, lambda v: v.tensor_scalar(stat[:, 12:13], stat[:, 11:12],
                                                        1.0 / 64.0, EPS, ALU.mult, ALU.add),
                           waits=[(Sc, sq1), (V, kb.ctr[V])])
                    vr = e(V, lambda v: v.reciprocal(stat[:, 13:14], stat[:, 12:13]))
                    sr = e(Sc, lambda s: s.activation(stat[:, 14:15], stat[:, 13:14], AF.Sqrt),
                           waits=[(V, vr)])
                    vr = e(V, lambda v, hh=hh, pR=pR: v.scalar_tensor_tensor(
                        ystg[:, hh * 64:(hh + 1) * 64], pR[:, 128:192], stat[:, 14:15],
                        hnb[:, hh * 64:(hh + 1) * 64], ALU.mult, ALU.mult), waits=[(Sc, sr)])
                    tm = e(Tn, lambda t, j=j, par=par, pS=pS: t.matmul(
                        pS[:, 0:64], kcct[:, j * 128:(j + 1) * 128],
                        xb[:, par * 64:(par + 1) * 64], start=True, stop=True),
                        waits=[(V, state_ev), (V, vap)])
                    state_ev = e(V, lambda v, j=j, hp=hp, pS=pS: v.scalar_tensor_tensor(
                        Sst[64 * hp:64 * hp + 64, j * 64:j * 64 + 64],
                        Sst[64 * hp:64 * hp + 64, j * 64:j * 64 + 64],
                        bCs[64 * hp:64 * hp + 64, j:j + 1],
                        pS[64 * hp:64 * hp + 64, 0:64], ALU.mult, ALU.add),
                        waits=[(Tn, tm), (V, kb.ctr[V])])
                    evac_par[par] = max(vr, state_ev)
                THEADS = kb.ctr[Tn]
                if st == 0 and ccx == 0:
                    dump("ystg0", lambda: ystg[:], [128, D], F32)
                    dump("xb0", lambda: xb[:], [128, 2 * 64], BF16)
                    dump("S_c0", lambda: Sst[:], [128, NJ * 64], F32)
                vyb = e(V, lambda v: v.tensor_copy(ybf[:, 0:512], ystg[:, 0:512]),
                        waits=[(V, kb.ctr[V])])
                vyb = e(V, lambda v: v.tensor_copy(ybf[:, 512:1024], ystg[:, 512:1024]))
                tty = 0
                for j in range(NJ):
                    tty = e(Tn, lambda t, j=j: t.transpose(
                        PT[:, j * 128:(j + 1) * 128], ybf[:, j * 128:(j + 1) * 128], ident[:]),
                        waits=(bankw(7, [(V, vyb)]) if j == 0 else ()))
                vyd = e(V, lambda v, cc=ccx: v.tensor_copy(
                    yd_st[:].rearrange("p (j t) -> p j t", j=NJ)[:, :, cc * 128:(cc + 1) * 128],
                    PT[:, 0:1024].rearrange("p (j t) -> p j t", j=NJ)), waits=[(Tn, tty)], nsw=True)
                bank_done(7, V)
            VSCAN = kb.ctr[V]

            if st == 0:
                dump("yd_st", lambda: yd_st[:], [128, NJ * ST], BF16)
                dump("Sst", lambda: Sst[:], [128, NJ * 64], F32)
            s1.close()
            s2 = contextlib.ExitStack()
            sb2 = lambda n, s, d, st=st: s2.enter_context(nc.sbuf_tensor(f"st{st}b_" + n, s, d))
            yattn = sb2("yattn", [128, NCHS * D], F32)
            y2d = sb2("y2d", [128, NJ * ST], BF16)
            prodd = sb2("prod", [128, 32 * ST], BF16)
            ybf2 = sb2("ybf2", [128, D], BF16)
            if upto == "a3":
                raise _Cut()
            kb.region = "ffn"
            SC_SCAN = kb.ctr[Sc]
            wv_guard = [0, 0]
            for hf in range(2):
                gv = 0
                tvcc = [0, 0, 0, 0]
                for i in range(NJ):
                    par = i % 2
                    gv = e(Gp, lambda g, i=i, h=hf, p=par: g.dma_start(
                        out=wrow[:, p * D:p * D + 512],
                        in_=w_o[i * 128:(i + 1) * 128, h * 512:h * 512 + 512]),
                        waits=[(Tn, wv_guard[par])] + ([(Tn, kb.ctr[Tn])] if i == 0 and hf == 0 else []))
                    for ccv in range(NCHS):
                        w0 = []
                        if i == 0:
                            w0 = bankw(ccv, [(Gp, gv), (V, VSCAN), (Sc, SC_SCAN)])
                        elif ccv == 0:
                            w0 = [(Gp, gv)]
                        tvcc[ccv] = e(Tn, lambda t, i=i, cc=ccv, p=par: t.matmul(
                            P[cc][:, 0:512], yd_st[:, i * ST + cc * 128:i * ST + cc * 128 + 128],
                            wrow[:, p * D:p * D + 512],
                            start=(i == 0), stop=(i == NJ - 1)), waits=w0)
                    wv_guard[par] = tvcc[NCHS - 1]
                for ccv in range(NCHS):
                    vgp = e(V, lambda v, cc=ccv, h=hf: v.tensor_mul(
                        scr1[:, 0:512], P[cc][:, 0:512],
                        gate_t[:, cc * D + h * 512:cc * D + h * 512 + 512]),
                        waits=[(Tn, tvcc[ccv]), (Sc, SC_GATE)], nsw=True)
                    bank_done(ccv, V)
                    e(V, lambda v, cc=ccv, h=hf: v.tensor_add(
                        yattn[:, cc * D + h * 512:cc * D + h * 512 + 512], scr1[:, 0:512],
                        lnx_t[:, cc * D + h * 512:cc * D + h * 512 + 512]), nsw=True)
            for ccx in range(NCHS):
                vy2 = e(V, lambda v, cc=ccx: v.tensor_mul(scr2[:, 0:D], yattn[:, cc * D:(cc + 1) * D],
                                                          yattn[:, cc * D:(cc + 1) * D]),
                        waits=[(Sc, kb.ctr[Sc])])
                vy2 = e(V, lambda v: v.reduce_sum(stat[:, 11:12], scr2[:, 0:D], AX.X))
                vy2 = e(V, lambda v: v.tensor_scalar(stat[:, 12:13], stat[:, 11:12], 1.0 / D, EPS,
                                                     ALU.mult, ALU.add))
                vy2 = e(V, lambda v: v.reciprocal(stat[:, 13:14], stat[:, 12:13]))
                sy2 = e(Sc, lambda s: s.activation(stat[:, 14:15], stat[:, 13:14], AF.Sqrt),
                        waits=[(V, vy2)])
                vy2 = e(V, lambda v, cc=ccx: v.scalar_tensor_tensor(
                    ybf2[:, 0:D], yattn[:, cc * D:(cc + 1) * D], stat[:, 14:15], ffnb[:, 0:D],
                    ALU.mult, ALU.mult), waits=[(Sc, sy2), (V, kb.ctr[V])])
                tt2 = 0
                for j in range(NJ):
                    tt2 = e(Tn, lambda t, j=j: t.transpose(
                        PT[:, j * 128:(j + 1) * 128], ybf2[:, j * 128:(j + 1) * 128], ident[:]),
                        waits=(bankw(7, [(V, vy2)]) if j == 0 else ()))
                vy2 = e(V, lambda v, cc=ccx: v.tensor_copy(
                    y2d[:].rearrange("p (j t) -> p j t", j=NJ)[:, :, cc * 128:(cc + 1) * 128],
                    PT[:, 0:1024].rearrange("p (j t) -> p j t", j=NJ)), waits=[(Tn, tt2)], nsw=True)
                bank_done(7, V)
            VY2 = kb.ctr[V]
            if st == 0:
                dump("yattn", lambda: yattn[:], [128, NCHS * D], F32)
                dump("y2d", lambda: y2d[:], [128, NJ * ST], BF16)
            gf = e(Gp, lambda g: wslice(g, fw1, wsA, 0, 0), waits=[(Tn, kb.ctr[Tn])])
            gf = e(Gp, lambda g: wslice(g, fw3, wsB, 0, 0))
            gfm = {0: gf}
            lastf = [0, 0]
            vpp = 0
            for fj in range(32):
                par = fj % 2
                if fj + 1 < 32:
                    gf = e(Gp, lambda g, fj=fj: wslice(g, fw1, wsA, (fj + 1) % 2, fj + 1),
                           waits=[(Tn, lastf[(fj + 1) % 2])])
                    gf = e(Gp, lambda g, fj=fj: wslice(g, fw3, wsB, (fj + 1) % 2, fj + 1))
                    gfm[fj + 1] = gf
                t1 = 0
                for i in range(NJ):
                    t1 = e(Tn, lambda t, i=i, p=par: t.matmul(
                        P[0][:, 0:512], wsA[:, p * 1024 + i * 128:p * 1024 + i * 128 + 128],
                        y2d[:, i * ST:i * ST + 512], start=(i == 0), stop=(i == NJ - 1)),
                        waits=(bankw(0, [(Gp, gfm[fj]), (V, VY2)]) if i == 0 else ()))
                t3 = 0
                for i in range(NJ):
                    t3 = e(Tn, lambda t, i=i, p=par: t.matmul(
                        P[1][:, 0:512], wsB[:, p * 1024 + i * 128:p * 1024 + i * 128 + 128],
                        y2d[:, i * ST:i * ST + 512], start=(i == 0), stop=(i == NJ - 1)),
                        waits=(bankw(1) if i == 0 else ()))
                sfs = e(Sc, lambda s: s.activation(scr1[:, 0:512], P[0][:, 0:512], AF.Silu),
                        waits=[(Tn, t1), (V, vpp)])
                bank_done(0, Sc)
                vpp = e(V, lambda v, fj=fj: v.tensor_mul(prodd[:, fj * ST:(fj + 1) * ST],
                                                         scr1[:, 0:512], P[1][:, 0:512]),
                        waits=[(Sc, sfs), (Tn, t3)], nsw=True)
                bank_done(1, V)
                lastf[par] = t3
            VPROD = kb.ctr[V]
            wv_guard = [0, 0]
            for hf in range(2):
                gv = 0
                tvcc = [0, 0, 0, 0]
                for fj in range(32):
                    par = fj % 2
                    gv = e(Gp, lambda g, fj=fj, h=hf, p=par: g.dma_start(
                        out=wrow[:, p * D:p * D + 512],
                        in_=fw2[fj * 128:(fj + 1) * 128, h * 512:h * 512 + 512]),
                        waits=[(Tn, wv_guard[par])] + ([(Tn, kb.ctr[Tn])] if fj == 0 and hf == 0 else []))
                    for ccv in range(NCHS):
                        w0 = []
                        if fj == 0:
                            w0 = bankw(ccv, [(Gp, gv), (V, VPROD), (Sc, kb.ctr[Sc])])
                        elif ccv == 0:
                            w0 = [(Gp, gv)]
                        tvcc[ccv] = e(Tn, lambda t, fj=fj, cc=ccv, p=par: t.matmul(
                            P[cc][:, 0:512], prodd[:, fj * ST + cc * 128:fj * ST + cc * 128 + 128],
                            wrow[:, p * D:p * D + 512],
                            start=(fj == 0), stop=(fj == 31)), waits=w0)
                    wv_guard[par] = tvcc[NCHS - 1]
                for ccv in range(NCHS):
                    e(V, lambda v, cc=ccv, h=hf: v.tensor_add(
                        yattn[:, cc * D + h * 512:cc * D + h * 512 + 512],
                        yattn[:, cc * D + h * 512:cc * D + h * 512 + 512], P[cc][:, 0:512]),
                      waits=[(Tn, tvcc[ccv])], nsw=True)
                    bank_done(ccv, V)
            for ccx in range(NCHS):
                gt = st * 4 + ccx
                par = gt % 2
                xd = e(Sy, lambda y, g=gt, p=par: y.dma_start(
                    out=xld[:, p * D:(p + 1) * D], in_=x_h[g * 128:(g + 1) * 128, :]),
                    waits=[(V, xld_guard[par])])
                vq1 = e(V, lambda v, cc=ccx, p=par: v.tensor_add(
                    scr2[:, 0:D], yattn[:, cc * D:(cc + 1) * D], xld[:, p * D:(p + 1) * D]),
                    waits=[(Sy, xd), (Sc, kb.ctr[Sc]), (V, kb.ctr[V])])
                vq1 = e(V, lambda v: v.tensor_copy(xnbf[:, 0:D], scr2[:, 0:D]),
                        waits=[(Sy, kb.ctr[Sy])])
                e(Sy, lambda y, gt=gt: y.dma_start(out=xnew_h[gt * 128:(gt + 1) * 128, :],
                                                   in_=xnbf[:, 0:D]), waits=[(V, vq1)])
                xld_guard[par] = vq1
                vq1 = e(V, lambda v: v.reduce_sum(stat[:, 0:1], scr2[:, 0:D], AX.X))
                vq1 = e(V, lambda v, gt=gt: v.tensor_scalar_mul(m2s[:, gt:gt + 1], stat[:, 0:1],
                                                                -1.0 / D))
                sq2 = e(Sc, lambda s, gt=gt: s.activation(scr2[:, 0:D], scr2[:, 0:D], AF.Square,
                                                          bias=m2s[:, gt:gt + 1], scale=1.0,
                                                          accum_out=stat[:, 2:3]),
                        waits=[(V, vq1)])
                vq1 = e(V, lambda v: v.tensor_scalar(stat[:, 3:4], stat[:, 2:3], 1.0 / D, 1e-5,
                                                     ALU.mult, ALU.add), waits=[(Sc, sq2)])
                vq1 = e(V, lambda v: v.reciprocal(stat[:, 4:5], stat[:, 3:4]))
                sq2 = e(Sc, lambda s, gt=gt: s.activation(i2s[:, gt:gt + 1], stat[:, 4:5], AF.Sqrt),
                        waits=[(V, vq1)])
                vq1 = e(V, lambda v: v.reduce_max(stat[:, 5:6], scr2[:, 0:D], AX.X),
                        waits=[(Sc, sq2)])
                sq2 = e(Sc, lambda s: s.activation(stat[:, 6:7], stat[:, 5:6], AF.Sqrt),
                        waits=[(V, vq1)])
                vq1 = e(V, lambda v, gt=gt: v.tensor_mul(stat[:, 7:8], stat[:, 6:7],
                                                         i2s[:, gt:gt + 1]), waits=[(Sc, sq2), (V, kb.ctr[V])])
                vq1 = e(V, lambda v: v.tensor_max(gmax1[:, 0:1], gmax1[:, 0:1], stat[:, 7:8]))
            s2.close()

        for st in range(NST):
            _emit_st(st)
            if upto == "st0":
                raise _Cut()
        if upto == "passA":
            raise _Cut()

     sW.close()
     kb.region = "passB"
     with contextlib.ExitStack() as sB:
        def sb(n, s, d):
            return sB.enter_context(nc.sbuf_tensor(n, s, d))

        xnb = sb("b_xnb", [128, 2 * D], BF16)
        xnb_guard = [0, 0]
        sxb = sb("b_sxb", [128, 1], F32)
        sxv = sb("b_sxv", [128, 1], F32)
        shb = sb("b_shb", [128, 1], F32)
        shv = sb("b_shv", [128, 1], F32)
        scsc = sb("b_scsc", [1, 8], F32)
        prodB = sb("b_prodB", [128, NI * T], BF16)
        b1 = contextlib.ExitStack()
        xqd = b1.enter_context(nc.sbuf_tensor("b_xqd", [128, NJ * T], BF16))
        xqt = b1.enter_context(nc.sbuf_tensor("b_xqt", [128, D], BF16))
        wnat = b1.enter_context(nc.sbuf_tensor("b_wnat", [128, 2 * D], F32))
        wnat2 = b1.enter_context(nc.sbuf_tensor("b_wnat2", [128, 2 * D], F32))
        wqi = b1.enter_context(nc.sbuf_tensor("b_wqi", [128, 2 * D], BF16))
        wqi2 = b1.enter_context(nc.sbuf_tensor("b_wqi2", [128, 2 * D], BF16))
        wqt = b1.enter_context(nc.sbuf_tensor("b_wqt", [128, 2 * NJ * 128], BF16))
        wqt2 = b1.enter_context(nc.sbuf_tensor("b_wqt2", [128, 2 * NJ * 128], BF16))
        swsc = b1.enter_context(nc.sbuf_tensor("b_swsc", [128, NI], F32))
        swsc2 = b1.enter_context(nc.sbuf_tensor("b_swsc2", [128, NI], F32))

        tg1 = e(Tn, lambda t: t.transpose(P[6][0:1, 0:128], gmax1[:, 0:1], identf[:]),
                waits=bankw(6, [(V, kb.ctr[V])]))
        vg1 = e(V, lambda v: v.memset(ccb[:], 0.0), waits=[(Tn, tg1)])
        vg1 = e(V, lambda v: v.reduce_max(ccb[0:1, 0:1], P[6][0:1, 0:128], AX.X), waits=[(V, vg1)])
        bank_done(6, V)
        gcc = e(Gp, lambda g: g.dma_start(out=cc1_i[0:1, :], in_=ccb[0:1, :]),
                waits=[(V, vg1)])
        gcc = e(Gp, lambda g: g.collective_compute(
            "AllReduce", ALU.max, replica_groups=[[0, 1, 2, 3, 4, 5, 6, 7]],
            ins=[cc1_i[0:1, :]], outs=[cc1_o[0:1, :]]),
            waits=[(Gp, kb.ctr[Gp])], sem=Tn)
        GCC1 = kb.ctr[Tn]
        gcc = e(Gp, lambda g: g.dma_start(out=ccb[0:1, :], in_=cc1_o[0:1, :]),
                waits=[(Tn, GCC1)])
        GCC1d = kb.ctr[Gp]
        sg1 = e(Sc, lambda s: s.activation(scsc[0:1, 0:2], clipb[0:1, 0:2], AF.Sigmoid),
                waits=[(Gp, GCC1d), (V, kb.ctr[V])])
        vg1 = e(V, lambda v: v.tensor_mul(scsc[0:1, 2:3], scsc[0:1, 0:1], ccb[0:1, 0:1]),
                waits=[(Sc, sg1)])
        vg1 = e(V, lambda v: v.tensor_scalar_mul(scsc[0:1, 2:3], scsc[0:1, 2:3], 1.0 / 127.0))
        vg1 = e(V, lambda v: v.reciprocal(scsc[0:1, 3:4], scsc[0:1, 2:3]))
        tb1 = e(Tn, lambda t: t.matmul(P[6][:, 0:2], onesf[0:1, :], scsc[0:1, 2:4],
                                       start=True, stop=True), waits=bankw(6, [(V, vg1)]))
        vg1 = e(V, lambda v: v.tensor_copy(sxv[:, 0:1], P[6][:, 0:1]), waits=[(Tn, tb1)])
        vg1 = e(V, lambda v: v.tensor_copy(sxb[:, 0:1], P[6][:, 1:2]))
        bank_done(6, V)
        VSX = kb.ctr[V]
        if upto == "ar1":
            raise _Cut()
        for gt in range(NT):
            par = gt % 2
            xnd = e(Sy, lambda y, gt=gt, p=par: y.dma_start(
                out=xnb[:, p * D:(p + 1) * D], in_=xnew_h[gt * 128:(gt + 1) * 128, :]),
                waits=[(V, xnb_guard[par])] + (kb.now(V, Sc, Tn, Sy) if gt == 0 else []))
            vq = e(V, lambda v, gt=gt, p=par: v.tensor_scalar(
                scr1[:, 0:D], xnb[:, p * D:(p + 1) * D], m2s[:, gt:gt + 1],
                i2s[:, gt:gt + 1], ALU.add, ALU.mult),
                waits=[(V, VSX), (Sc, kb.ctr[Sc]), (Sy, xnd)])
            xnb_guard[par] = kb.ctr[V]
            vq = e(V, lambda v: v.tensor_scalar(scr1[:, 0:D], scr1[:, 0:D], sxb[:, 0:1],
                                                MAGIC, ALU.mult, ALU.add))
            vq = e(V, lambda v: v.tensor_scalar(scr1[:, 0:D], scr1[:, 0:D], -MAGIC, None, ALU.add))
            vq = e(V, lambda v: v.tensor_scalar(xqt[:, 0:D], scr1[:, 0:D], -128.0, 127.0,
                                                ALU.max, ALU.min))
            ttx = 0
            for j in range(NJ):
                ttx = e(Tn, lambda t, j=j: t.transpose(
                    PT[:, j * 128:(j + 1) * 128], xqt[:, j * 128:(j + 1) * 128], ident[:]),
                    waits=(bankw(7, [(V, vq)]) if j == 0 else ()))
            vq = e(V, lambda v, gt=gt: v.tensor_copy(
                xqd[:].rearrange("p (j t) -> p j t", j=NJ)[:, :, gt * 128:(gt + 1) * 128],
                PT[:, 0:1024].rearrange("p (j t) -> p j t", j=NJ)), waits=[(Tn, ttx)], nsw=True)
            bank_done(7, V)
        VXQ = kb.ctr[V]
        if upto == "xq":
            raise _Cut()

        gq = e(Gp, lambda g: g.dma_start(out=wnat[:, 0:D], in_=wg[0:128, :]),
               waits=[(V, kb.ctr[V]), (Sc, kb.ctr[Sc]), (Tn, kb.ctr[Tn])])
        gq = e(Gp, lambda g: g.dma_start(out=wnat2[:, 0:D], in_=wu[0:128, :]))
        gqm = {0: gq}
        wq_mm_done = [0, 0]
        prod_read = [0]
        for ic in range(NI):
            par = ic % 2
            if ic + 1 < NI:
                gq = e(Gp, lambda g, ic=ic: g.dma_start(
                    out=wnat[:, ((ic + 1) % 2) * D:((ic + 1) % 2) * D + D],
                    in_=wg[(ic + 1) * 128:(ic + 2) * 128, :]),
                    waits=[(V, kb.ctr[V])])
                gq = e(Gp, lambda g, ic=ic: g.dma_start(
                    out=wnat2[:, ((ic + 1) % 2) * D:((ic + 1) % 2) * D + D],
                    in_=wu[(ic + 1) * 128:(ic + 2) * 128, :]))
                gqm[ic + 1] = gq
            for (wn, wi, wt, ssc_, bank) in [(wnat, wqi, wqt, swsc, 0), (wnat2, wqi2, wqt2, swsc2, 1)]:
                vw = e(V, lambda v, ic=ic, p=par, wn=wn: v.reduce_max(
                    stat[:, 0:1], wn[:, p * D:(p + 1) * D], AX.X, apply_absolute_value=True),
                    waits=[(Gp, gqm[ic]), (V, kb.ctr[V]), (Tn, wq_mm_done[par])])
                vw = e(V, lambda v, ic=ic, ssc_=ssc_: v.tensor_scalar_mul(
                    ssc_[:, ic:ic + 1], stat[:, 0:1], 1.0 / 127.0))
                vw = e(V, lambda v: v.reciprocal(stat[:, 1:2], stat[:, 0:1]))
                vw = e(V, lambda v: v.tensor_scalar_mul(stat[:, 1:2], stat[:, 1:2], 127.0))
                vw = e(V, lambda v, p=par, wn=wn: v.tensor_scalar(
                    scr1[:, 0:D], wn[:, p * D:(p + 1) * D], stat[:, 1:2], MAGIC,
                    ALU.mult, ALU.add), waits=[(V, vw)])
                vw = e(V, lambda v: v.tensor_scalar(scr1[:, 0:D], scr1[:, 0:D], -MAGIC, None, ALU.add))
                vw = e(V, lambda v, p=par, wi=wi: v.tensor_scalar(
                    wi[:, p * D:(p + 1) * D], scr1[:, 0:D], -128.0, 127.0, ALU.max, ALU.min))
                ttw = 0
                for j in range(NJ):
                    ttw = e(Tn, lambda t, j=j, p=par, wi=wi: t.transpose(
                        PT[:, j * 128:(j + 1) * 128], wi[:, p * D + j * 128:p * D + j * 128 + 128],
                        ident[:]), waits=(bankw(7, [(V, vw)]) if j == 0 else ()))
                vw = e(V, lambda v, p=par, wt=wt: v.tensor_copy(
                    wt[:, p * 1024:(p + 1) * 1024], PT[:, 0:1024]), waits=[(Tn, ttw)], nsw=True)
                bank_done(7, V)
                vw = e(V, lambda v, ic=ic, ssc_=ssc_, b=bank: v.scalar_tensor_tensor(
                    stat[:, 2 + b:3 + b], sxv[:, 0:1], 1.0, ssc_[:, ic:ic + 1], ALU.mult, ALU.mult),
                    waits=[(V, vw), (Sc, kb.ctr[Sc])])
            VWQ = kb.ctr[V]
            for stq in range(T // 512):
                tg_ = 0
                for j in range(NJ):
                    tg_ = e(Tn, lambda t, j=j, p=par, stq=stq: t.matmul(
                        P[0][:, 0:512], wqt[:, p * 1024 + j * 128:p * 1024 + j * 128 + 128],
                        xqd[:, j * T + stq * 512:j * T + stq * 512 + 512],
                        start=(j == 0), stop=(j == NJ - 1)),
                        waits=(bankw(0, [(V, VWQ), (V, VXQ)]) if j == 0 else ()))
                tu_ = 0
                for j in range(NJ):
                    tu_ = e(Tn, lambda t, j=j, p=par, stq=stq: t.matmul(
                        P[1][:, 0:512], wqt2[:, p * 1024 + j * 128:p * 1024 + j * 128 + 128],
                        xqd[:, j * T + stq * 512:j * T + stq * 512 + 512],
                        start=(j == 0), stop=(j == NJ - 1)),
                        waits=(bankw(1) if j == 0 else ()))
                sgt = e(Sc, lambda s: s.activation(scr1[:, 0:512], P[0][:, 0:512], AF.Silu,
                                                   scale=stat[:, 2:3]),
                        waits=[(Tn, tg_), (V, VWQ), (V, prod_read[0])])
                bank_done(0, Sc)
                vgt = e(V, lambda v: v.tensor_scalar(scr2[:, 0:512], P[1][:, 0:512], stat[:, 3:4],
                                                     None, ALU.mult), waits=[(Tn, tu_), (Sc, sgt)])
                bank_done(1, V)
                vgt = e(V, lambda v, ic=ic, stq=stq: v.tensor_mul(
                    prodB[:, ic * T + stq * 512:ic * T + stq * 512 + 512], scr1[:, 0:512],
                    scr2[:, 0:512]), nsw=True)
                prod_read[0] = vgt
                vgt = e(V, lambda v, ic=ic, stq=stq: v.reduce_max(
                    stat[:, 4:5], prodB[:, ic * T + stq * 512:ic * T + stq * 512 + 512], AX.X,
                    apply_absolute_value=True))
                vgt = e(V, lambda v, ic=ic: v.tensor_max(gmax2[:, ic:ic + 1], gmax2[:, ic:ic + 1],
                                                         stat[:, 4:5]))
                wq_mm_done[par] = tu_
        if upto == "gateup":
            raise _Cut()
        dump("xqd", lambda: xqd[:], [128, NJ * T], BF16)
        dump("prodB", lambda: prodB[:], [128, NI * T], BF16)
        b1.close()
        vg2 = e(V, lambda v: v.reduce_max(stat[:, 0:1], gmax2[:, 0:NI], AX.X),
                waits=[(V, kb.ctr[V])])
        tg2 = e(Tn, lambda t: t.transpose(P[6][0:1, 0:128], stat[:, 0:1], identf[:]),
                waits=bankw(6, [(V, kb.ctr[V])]))
        vg2 = e(V, lambda v: v.memset(ccb[:], 0.0), waits=[(Tn, tg2)])
        vg2 = e(V, lambda v: v.reduce_max(ccb[0:1, 0:1], P[6][0:1, 0:128], AX.X), waits=[(V, vg2)])
        bank_done(6, V)
        gc2 = e(Gp, lambda g: g.dma_start(out=cc2_i[0:1, :], in_=ccb[0:1, :]), waits=[(V, vg2)])
        gc2 = e(Gp, lambda g: g.collective_compute(
            "AllReduce", ALU.max, replica_groups=[[0, 1, 2, 3, 4, 5, 6, 7]],
            ins=[cc2_i[0:1, :]], outs=[cc2_o[0:1, :]]),
            waits=[(Gp, kb.ctr[Gp])], sem=Tn)
        GCC2 = kb.ctr[Tn]
        gc2 = e(Gp, lambda g: g.dma_start(out=ccb[0:1, :], in_=cc2_o[0:1, :]), waits=[(Tn, GCC2)])
        GCC2d = kb.ctr[Gp]
        vg2 = e(V, lambda v: v.tensor_mul(scsc[0:1, 4:5], scsc[0:1, 1:2], ccb[0:1, 0:1]),
                waits=[(Gp, GCC2d), (V, kb.ctr[V])])
        vg2 = e(V, lambda v: v.tensor_scalar_mul(scsc[0:1, 4:5], scsc[0:1, 4:5], 1.0 / 127.0))
        vg2 = e(V, lambda v: v.reciprocal(scsc[0:1, 5:6], scsc[0:1, 4:5]))
        tb2 = e(Tn, lambda t: t.matmul(P[6][:, 0:2], onesf[0:1, :], scsc[0:1, 4:6],
                                       start=True, stop=True), waits=bankw(6, [(V, vg2)]))
        vg2 = e(V, lambda v: v.tensor_copy(shv[:, 0:1], P[6][:, 0:1]), waits=[(Tn, tb2)])
        vg2 = e(V, lambda v: v.tensor_copy(shb[:, 0:1], P[6][:, 1:2]))
        bank_done(6, V)
        VSH = kb.ctr[V]
        if upto == "ar2":
            raise _Cut()
        for ic in range(NI):
            for q4 in range(T // 512):
                off = ic * T + q4 * 512
                vq = e(V, lambda v, off=off: v.tensor_scalar(
                    scr1[:, 0:512], prodB[:, off:off + 512], shb[:, 0:1], MAGIC,
                    ALU.mult, ALU.add), waits=[(V, VSH)] if (ic == 0 and q4 == 0) else ())
                vq = e(V, lambda v: v.tensor_scalar(scr1[:, 0:512], scr1[:, 0:512], -MAGIC,
                                                    None, ALU.add))
                vq = e(V, lambda v, off=off: v.tensor_scalar(
                    prodB[:, off:off + 512], scr1[:, 0:512], -128.0, 127.0, ALU.max, ALU.min))
        VH2 = kb.ctr[V]
        if upto == "prodq":
            raise _Cut()
        wdT = sb("b_wdT", [128, NI * D], BF16)
        sdb = sb("b_sdb", [128, D], F32)
        sdr = sb("b_sdr", [8, 128], F32)
        sdrow = sb("b_sdrow", [1, D], F32)
        wdn = sb("b_wdn", [128, II], F32)
        wdq = sb("b_wdq", [128, II], BF16)
        swd = sb("b_swd", [128, NJ], F32)
        gwd_mark = 0
        twd_done = 0
        for jh in range(NJ):
            gwd = e(Gp, lambda g, jh=jh: g.dma_start(out=wdn[:, 0:II],
                                                     in_=wd[jh * 128:(jh + 1) * 128, :]),
                    waits=[(V, kb.ctr[V]), (Tn, twd_done)])
            vw = e(V, lambda v, jh=jh: v.reduce_max(stat[:, 0:1], wdn[:, 0:II], AX.X,
                                                    apply_absolute_value=True),
                   waits=[(Gp, kb.ctr[Gp])])
            vw = e(V, lambda v, jh=jh: v.tensor_scalar_mul(swd[:, jh:jh + 1], stat[:, 0:1],
                                                           1.0 / 127.0))
            vw = e(V, lambda v: v.reciprocal(stat[:, 1:2], stat[:, 0:1]))
            vw = e(V, lambda v: v.tensor_scalar_mul(stat[:, 1:2], stat[:, 1:2], 127.0))
            for q4 in range(3):
                pass
            vw2 = vw
            for piece in range(4):
                off = piece * 704
                vw2 = e(V, lambda v, off=off: v.tensor_scalar(
                    scr1[:, 0:704], wdn[:, off:off + 704], stat[:, 1:2], MAGIC, ALU.mult, ALU.add),
                    waits=[(V, vw)] if piece == 0 else ())
                vw2 = e(V, lambda v: v.tensor_scalar(scr1[:, 0:704], scr1[:, 0:704], -MAGIC,
                                                     None, ALU.add))
                vw2 = e(V, lambda v, off=off: v.tensor_scalar(
                    wdq[:, off:off + 704], scr1[:, 0:704], -128.0, 127.0, ALU.max, ALU.min))
            ttd = 0
            for bi in range(NI):
                bw = bankw(7, [(V, vw2)]) if bi == 0 else (
                    [(V, kb.ctr[V])] if bi % 8 == 0 else [])
                ttd = e(Tn, lambda t, bi=bi: t.transpose(
                    PT[:, (bi % 8) * 128:(bi % 8) * 128 + 128], wdq[:, bi * 128:(bi + 1) * 128],
                    ident[:]), waits=bw)
                if bi % 8 == 7 or bi == NI - 1:
                    lo = (bi // 8) * 8
                    vw2 = e(V, lambda v, lo=lo, bi=bi, jh=jh: v.tensor_copy(
                        wdT[:].rearrange("p (i h) -> p i h", i=NI)[:, lo:bi + 1, jh * 128:(jh + 1) * 128],
                        PT[:, 0:(bi + 1 - lo) * 128].rearrange("p (i h) -> p i h", h=128)),
                        waits=[(Tn, ttd)], nsw=True)
                    bank_done(7, V)
            twd_done = kb.ctr[Tn]
        tsd = e(Tn, lambda t: t.transpose(P[6][0:8, 0:128], swd[:, 0:NJ], identf[:]),
                waits=bankw(6, [(V, kb.ctr[V])]))
        vsd = e(V, lambda v: v.tensor_copy(sdr[0:8, 0:128], P[6][0:8, 0:128]), waits=[(Tn, tsd)])
        bank_done(6, V)
        gsd = e(Gp, lambda g: g.dma_start(out=sdrow[0:1, :].rearrange("a (j p) -> a j p", j=NJ),
                                          in_=sdr[0:8, 0:128].rearrange("j p -> 1 j p") if False else sdr[0:8, 0:128]),
                waits=[(V, vsd)])
        GSD = kb.ctr[Gp]
        for hf in range(2):
            tsd = e(Tn, lambda t, h=hf: t.matmul(P[6][:, 0:512], onesf[0:1, :],
                                                 sdrow[0:1, h * 512:h * 512 + 512],
                                                 start=True, stop=True),
                    waits=bankw(6, [(Gp, GSD)]))
            vsd = e(V, lambda v, h=hf: v.tensor_scalar(
                sdb[:, h * 512:h * 512 + 512], P[6][:, 0:512], shv[:, 0:1], None, ALU.mult),
                waits=[(Tn, tsd)])
            bank_done(6, V)
        VSD = kb.ctr[V]
        if upto == "wd":
            raise _Cut()
        for gt in range(NT):
            xnd = e(Sy, lambda y, gt=gt, p=(gt % 2): y.dma_start(
                out=xnb[:, p * D:(p + 1) * D], in_=xnew_h[gt * 128:(gt + 1) * 128, :]),
                waits=[(V, xnb_guard[gt % 2])])
            for hf in range(2):
                td = 0
                for bi in range(NI):
                    td = e(Tn, lambda t, bi=bi, gt=gt, h=hf: t.matmul(
                        P[3][:, 0:512], prodB[:, bi * T + gt * 128:bi * T + gt * 128 + 128],
                        wdT[:, bi * D + h * 512:bi * D + h * 512 + 512],
                        start=(bi == 0), stop=(bi == NI - 1)),
                        waits=(bankw(3, [(V, VSD), (V, VH2)]) if bi == 0 else ()))
                vd = e(V, lambda v, h=hf: v.tensor_mul(scr1[:, h * 512:h * 512 + 512],
                                                       P[3][:, 0:512], sdb[:, h * 512:h * 512 + 512]),
                       waits=[(Tn, td)] + ([("out", gt * 16)] if hf == 0 else []), nsw=True)
                bank_done(3, V)
                vd = e(V, lambda v, gt=gt, h=hf, p=(gt % 2): v.tensor_add(
                    scr1[:, h * 512:h * 512 + 512], scr1[:, h * 512:h * 512 + 512],
                    xnb[:, p * D + h * 512:p * D + h * 512 + 512]),
                    waits=[(Sy, xnd)] if hf == 0 else [], nsw=True)
            xnb_guard[gt % 2] = kb.ctr[V]
            e(Sy, lambda y, gt=gt: y.dma_start(out=out_h[gt * 128:(gt + 1) * 128, :],
                                               in_=scr1[:, 0:D]),
              waits=[(V, kb.ctr[V])], sem="out")

    except _Cut:
        pass
    kb.wait(Sy, [("out", kb.ctr["out"])])

    with nc.Block() as block:
        @block.sync
        def _(y):
            for r in kb.sched[Sy]:
                r(y)

        @block.vector
        def _(v):
            for r in kb.sched[V]:
                r(v)

        @block.scalar
        def _(s):
            for r in kb.sched[Sc]:
                r(s)

        @block.tensor
        def _(t):
            for r in kb.sched[Tn]:
                r(t)

        @block.gpsimd
        def _(g):
            for r in kb.sched[Gp]:
                r(g)
    if upto is None:
        kb.es.close()
    return nc


_CONSTS = None


def _consts():
    global _CONSTS
    if _CONSTS is None:
        _CONSTS = {
            "c_ident": np.eye(128, dtype=ml_dtypes.bfloat16),
            "c_masku": np.triu(np.ones((128, 128), np.uint8), 1),
            "c_maskui": np.triu(np.ones((128, 128), np.uint8), 0),
            "c_maskl": np.tril(np.ones((128, 128), np.uint8), -1),
            "c_ones": np.ones((1, 128), dtype=ml_dtypes.bfloat16),
            "c_onesf": np.ones((1, 128), dtype=np.float32),
            "c_identf": np.eye(128, dtype=np.float32),
        }
        E = np.zeros((16, NJ * 128), dtype=ml_dtypes.bfloat16)
        F = np.zeros((128, NJ * 16), dtype=ml_dtypes.bfloat16)
        for j in range(NJ):
            for m in range(128):
                h = 2 * j + (1 if m >= 64 else 0)
                E[h, j * 128 + m] = 1
                F[m, j * 16 + h] = 1
        _CONSTS["c_E"] = E
        _CONSTS["c_F"] = F
    return _CONSTS


_NC_CACHE = {}


def kernel(**inputs):
    T_ = inputs["L0_attn_x"].shape[1]
    if T_ not in _NC_CACHE:
        _NC_CACHE[T_] = build(T_, serial=SELF_WAIT_V)
    nc = _NC_CACHE[T_]
    consts = _consts()
    x = np.ascontiguousarray(inputs["L0_attn_x"])
    B = x.shape[0]
    base = {
        "q_w": inputs["L0_attn_q_w"], "k_w": inputs["L0_attn_k_w"],
        "v_w": inputs["L0_attn_v_w"], "alpha_w": inputs["L0_attn_alpha_w"],
        "alpha_b": inputs["L0_attn_alpha_b"], "beta_w": inputs["L0_attn_beta_w"],
        "beta_b": inputs["L0_attn_beta_b"], "out_w": inputs["L0_attn_out_w"],
        "conv_w": np.ascontiguousarray(inputs["L0_attn_conv_w"].reshape(D, 4)),
        "conv_b": inputs["L0_attn_conv_b"], "norm_in_w": inputs["L0_attn_norm_in_w"],
        "head_norm_w": np.ascontiguousarray(inputs["L0_attn_head_norm_w"].reshape(H * 64)),
        "u1_w": inputs["L0_attn_u1_w"], "u1_b": inputs["L0_attn_u1_b"],
        "u2_w": inputs["L0_attn_u2_w"], "u2_b": inputs["L0_attn_u2_b"],
        "ff_norm_w": inputs["L0_attn_ff_norm_w"], "ff_w1": inputs["L0_attn_ff_w1"],
        "ff_w3": inputs["L0_attn_ff_w3"], "ff_w2": inputs["L0_attn_ff_w2"],
        "clip_x": inputs["L0_ffn_clip_x"], "clip_down": inputs["L0_ffn_clip_down"],
        "w_gate": inputs["L0_ffn_w_gate"], "w_up": inputs["L0_ffn_w_up"],
        "w_down": inputs["L0_ffn_w_down"],
    }
    base.update(consts)
    in_maps = []
    for b in range(B):
        m = dict(base)
        m["x"] = x[b]
        in_maps.append(m)
    res = run_bass_kernel_spmd(nc, in_maps, core_ids=list(range(B)))
    out = np.stack([res.results[b]["out"] for b in range(B)], axis=0)
    return out.astype(np.float32)


# version 06 (final)
# speedup vs baseline: 1.1816x; 1.1816x over previous
import contextlib

import numpy as np
import ml_dtypes

import concourse.bass as bass
import concourse.mybir as mybir
from concourse.bass_utils import run_bass_kernel_spmd

F32 = mybir.dt.float32
BF16 = mybir.dt.bfloat16
U8 = mybir.dt.uint8
AF = mybir.ActivationFunctionType
ALU = mybir.AluOpType
AX = mybir.AxisListType

D = 1024
H = 16
FF = 4096
RR = 64
II = 2816
NI = II // 128
EPS = 1e-6
C = 128
NJ = 8
MAGIC = float(1.5 * 2 ** 23)

V, Tn, Sc, Sy, Gp = "vector", "tensor", "scalar", "sync", "gpsimd"


SELF_WAIT = "self-wait"
SELF_WAIT_VS = "self-wait-vs"
SELF_WAIT_V = "self-wait-v"
SELF_WAIT_V1 = "self-wait-v-lag1"


class KB:
    def __init__(self, nc):
        self.nc = nc
        self.es = contextlib.ExitStack()
        self.sched = {k: [] for k in (V, Tn, Sc, Sy, Gp)}
        self.ctr = {k: 0 for k in (V, Tn, Sc, Sy, Gp, "out")}
        self.sem = {k: self.es.enter_context(nc.semaphore("sem_" + k))
                    for k in (V, Tn, Sc, Sy, Gp, "out")}
        self.bank_guard = {}

    def sb(self, name, shape, dt):
        return self.es.enter_context(self.nc.sbuf_tensor(name, shape, dt))

    def ps(self, name, shape, dt=F32):
        return self.es.enter_context(self.nc.psum_tensor(name, shape, dt))

    serial = False
    region = "init"

    def e(self, eng, fn, waits=(), sem=None, nsw=False):
        semk = sem if sem is not None else eng
        inc = 16 if semk in (Sy, Gp, "out") else 1
        my_sem = self.sem[semk]
        if self.serial is True or (
                self.serial and self.serial is not SELF_WAIT and self.region in self.serial):
            waits = list(waits) + [(k, self.ctr[k]) for k in (V, Tn, Sc, Sy, Gp, "out")]
        elif self.serial is SELF_WAIT and eng in (V, Tn, Sc):
            waits = list(waits) + [(eng, self.ctr[eng])]
        elif self.serial is SELF_WAIT_VS and eng in (V, Sc):
            waits = list(waits) + [(eng, self.ctr[eng])]
        elif self.serial is SELF_WAIT_V and eng == V:
            if not nsw and self.region not in ("init", "proj"):
                waits = list(waits) + [(eng, self.ctr[eng])]
        elif self.serial is SELF_WAIT_V1 and eng == V:
            waits = list(waits) + [(eng, self.ctr[eng] - 1)]
        wl = tuple((k, v) for (k, v) in waits if v > 0)

        def run(engine, fn=fn, wl=wl, my_sem=my_sem, inc=inc):
            for wk, wv in wl:
                engine.wait_ge(self.sem[wk], wv)
            fn(engine).then_inc(my_sem, inc)

        self.sched[eng].append(run)
        self.ctr[semk] += inc
        return self.ctr[semk]

    def wait(self, eng, waits):
        wl = tuple((k, v) for (k, v) in waits if v > 0)

        def run(engine, wl=wl):
            for wk, wv in wl:
                engine.wait_ge(self.sem[wk], wv)

        self.sched[eng].append(run)

    def now(self, *engs):
        return [(k, self.ctr[k]) for k in engs]


class _Cut(Exception):
    pass


def raise_cut(e):
    raise _Cut()


def build(T=2048, dbg=False, upto=None, serial=False):
    nc = bass.Bass("TRN2", target_bir_lowering=False, debug=False,
                   detect_race_conditions=False)
    kb = KB(nc)
    kb.serial = serial
    e = kb.e
    NT = T // 128
    NST = T // 512
    ST = 512
    NCHS = ST // C

    def din(name, shape, dt=F32):
        return nc.dram_tensor(name, shape, dt, kind="ExternalInput").ap()

    x_h = din("x", [T, D])
    w_q, w_k, w_v = din("q_w", [D, D]), din("k_w", [D, D]), din("v_w", [D, D])
    w_a, b_a = din("alpha_w", [D, D]), din("alpha_b", [D])
    w_b, b_b = din("beta_w", [D, H]), din("beta_b", [H])
    w_o = din("out_w", [D, D])
    w_cv, b_cv = din("conv_w", [D, 4]), din("conv_b", [D])
    nin_w = din("norm_in_w", [D])
    hn_w = din("head_norm_w", [H * 64])
    u1w, u1b = din("u1_w", [D, RR]), din("u1_b", [RR])
    u2w, u2b = din("u2_w", [RR, D]), din("u2_b", [D])
    ffn_w = din("ff_norm_w", [D])
    fw1, fw3, fw2 = din("ff_w1", [D, FF]), din("ff_w3", [D, FF]), din("ff_w2", [FF, D])
    clipx, clipd = din("clip_x", [1]), din("clip_down", [1])
    wg, wu, wd = din("w_gate", [II, D]), din("w_up", [II, D]), din("w_down", [D, II])
    c_id = din("c_ident", [128, 128], BF16)
    c_mu = din("c_masku", [128, 128], U8)
    c_mui = din("c_maskui", [128, 128], U8)
    c_ml = din("c_maskl", [128, 128], U8)
    c_E = din("c_E", [16, NJ * 128], BF16)
    c_F = din("c_F", [128, NJ * 16], BF16)
    c_ones = din("c_ones", [1, 128], BF16)
    c_onesf = din("c_onesf", [1, 128], F32)
    c_idf = din("c_identf", [128, 128], F32)

    out_h = nc.dram_tensor("out", [T, D], F32, kind="ExternalOutput").ap()
    _dbg_seen = {}

    def dump(name, ap_fn, shape, dt):
        if not dbg or name in _dbg_seen:
            return
        _dbg_seen[name] = True
        dh = nc.dram_tensor("dbg_" + name, shape, dt, kind="ExternalOutput").ap()
        kb.e(Sy, lambda y: y.dma_start(out=dh[:, :], in_=ap_fn()),
             waits=[(V, kb.ctr[V]), (Sc, kb.ctr[Sc]), (Tn, kb.ctr[Tn])], sem="out")
    cc1_i = nc.dram_tensor("cc1_in", [1, 16], F32, kind="Internal").ap()
    cc1_o = nc.dram_tensor("cc1_out", [1, 16], F32, kind="Internal").ap()
    cc2_i = nc.dram_tensor("cc2_in", [1, 16], F32, kind="Internal").ap()
    cc2_o = nc.dram_tensor("cc2_out", [1, 16], F32, kind="Internal").ap()
    xnew_h = nc.dram_tensor("xnew_scr", [T, D], BF16, kind="Internal").ap()

    P = [kb.ps(f"P{i}", [128, 512], F32) for i in range(7)]
    PT = kb.ps("PTr", [128, 1024], BF16)

    def bankw(i, extra=()):
        g = kb.bank_guard.get(i)
        w = list(extra)
        if g:
            w.append(g)
        return w

    def bank_done(i, eng):
        kb.bank_guard[i] = (eng, kb.ctr[eng])

    ident = kb.sb("s_ident", [128, 128], BF16)
    masku = kb.sb("s_masku", [128, 128], U8)
    maskui = kb.sb("s_maskui", [128, 128], U8)
    maskl = kb.sb("s_maskl", [128, 128], U8)
    Emat = kb.sb("s_E", [16, NJ * 128], BF16)
    Fmat = kb.sb("s_F", [128, NJ * 16], BF16)
    onesb = kb.sb("s_onesb", [1, 128], BF16)
    onesf = kb.sb("s_onesf", [1, 128], F32)
    identf = kb.sb("s_identf", [128, 128], F32)
    Sst = kb.sb("s_S", [128, NJ * 64], F32)
    carry = kb.sb("s_carry", [128, NJ], F32)
    m2s = kb.sb("s_m2s", [128, NT], F32)
    i2s = kb.sb("s_i2s", [128, NT], F32)
    gmax1 = kb.sb("s_gmax1", [128, 1], F32)
    gmax2 = kb.sb("s_gmax2", [128, NI], F32)
    clipb = kb.sb("s_clipb", [1, 2], F32)
    ccb = kb.sb("s_ccb", [1, 16], F32)
    stat = kb.sb("s_stat", [128, 16], F32)
    scr1 = kb.sb("s_scr1", [128, D], F32)
    scr2 = kb.sb("s_scr2", [128, D], F32)
    sW = contextlib.ExitStack()
    convw = sW.enter_context(nc.sbuf_tensor("s_convw", [128, 4 * NJ], F32))
    convb = sW.enter_context(nc.sbuf_tensor("s_convb", [128, NJ], F32))
    alphab = sW.enter_context(nc.sbuf_tensor("s_alphab", [128, NJ], F32))
    ninw = sW.enter_context(nc.sbuf_tensor("s_ninw", [128, NJ], F32))
    u1b_s = sW.enter_context(nc.sbuf_tensor("s_u1b", [64, 1], F32))
    betab = sW.enter_context(nc.sbuf_tensor("s_betab", [16, 1], F32))
    u2bb = sW.enter_context(nc.sbuf_tensor("s_u2bb", [128, D], BF16))
    ffnb = sW.enter_context(nc.sbuf_tensor("s_ffnb", [128, D], BF16))
    hnb = sW.enter_context(nc.sbuf_tensor("s_hnb", [128, D], BF16))
    u1w_s = sW.enter_context(nc.sbuf_tensor("s_u1w", [128, NJ * 64], BF16))
    u2w_s = sW.enter_context(nc.sbuf_tensor("s_u2w", [64, D], BF16))
    betaw = sW.enter_context(nc.sbuf_tensor("s_betaw", [128, NJ * 16], BF16))
    rowbuf = sW.enter_context(nc.sbuf_tensor("s_rowbuf", [1, 3 * D], BF16))

    ld = 0
    for a, b in [(ident, c_id), (masku, c_mu), (maskui, c_mui), (maskl, c_ml),
                 (Emat, c_E), (Fmat, c_F), (onesb, c_ones), (onesf, c_onesf),
                 (identf, c_idf)]:
        ld = e(Sy, lambda y, a=a, b=b: y.dma_start(out=a[:], in_=b[:, :]))
    ld = e(Sy, lambda y: y.dma_start(out=convw[:].rearrange("p (j k) -> p j k", j=NJ),
                                     in_=w_cv.rearrange("(j p) k -> p j k", p=128)))
    ld = e(Sy, lambda y: y.dma_start(out=convb[:], in_=b_cv.rearrange("(j p) -> p j", p=128), allow_slow_non_contiguous=True))
    ld = e(Sy, lambda y: y.dma_start(out=alphab[:], in_=b_a.rearrange("(j p) -> p j", p=128), allow_slow_non_contiguous=True))
    ld = e(Sy, lambda y: y.dma_start(out=ninw[:], in_=nin_w.rearrange("(j p) -> p j", p=128), allow_slow_non_contiguous=True))
    ld = e(Sy, lambda y: y.dma_start(out=u1b_s[:], in_=u1b.rearrange("(p o) -> p o", o=1)))
    ld = e(Sy, lambda y: y.dma_start(out=betab[:], in_=b_b.rearrange("(p o) -> p o", o=1)))
    ld = e(Sy, lambda y: y.dma_start(out=clipb[0:1, 0:1], in_=clipx.rearrange("(a b) -> a b", a=1)))
    ld = e(Sy, lambda y: y.dma_start(out=clipb[0:1, 1:2], in_=clipd.rearrange("(a b) -> a b", a=1)))
    ldg = e(Gp, lambda g: g.dma_start(out=u1w_s[:].rearrange("p (i r) -> p i r", i=NJ),
                                      in_=u1w.rearrange("(i p) r -> p i r", p=128)))
    ldg = e(Gp, lambda g: g.dma_start(out=u2w_s[:], in_=u2w[:, :]))
    ldg = e(Gp, lambda g: g.dma_start(out=betaw[:].rearrange("p (i h) -> p i h", i=NJ),
                                      in_=w_b.rearrange("(i p) h -> p i h", p=128)))
    ldg = e(Gp, lambda g: g.dma_start(out=rowbuf[0:1, 0:D], in_=u2b.rearrange("(a d) -> a d", a=1)))
    ldg = e(Gp, lambda g: g.dma_start(out=rowbuf[0:1, D:2 * D], in_=ffn_w.rearrange("(a d) -> a d", a=1)))
    ldg = e(Gp, lambda g: g.dma_start(out=rowbuf[0:1, 2 * D:3 * D], in_=hn_w.rearrange("(a d) -> a d", a=1)))
    LDC, LDG = ld, ldg

    for idx, dst in [(0, u2bb), (1, ffnb), (2, hnb)]:
        for hf in range(2):
            tm = e(Tn, lambda t, i=idx, hf=hf: t.matmul(
                P[0][:, 0:512], onesb[0:1, :],
                rowbuf[0:1, i * D + hf * 512: i * D + hf * 512 + 512],
                start=True, stop=True),
                waits=bankw(0, [(Sy, LDC), (Gp, LDG)]))
            e(V, lambda v, d=dst, hf=hf: v.tensor_copy(d[:, hf * 512:hf * 512 + 512], P[0][:, 0:512]),
              waits=[(Tn, tm)])
            bank_done(0, V)
    e(V, lambda v: v.tensor_mul(
        convw[:].rearrange("p (j k) -> p j k", j=NJ),
        convw[:].rearrange("p (j k) -> p j k", j=NJ),
        ninw[:].rearrange("p (j o) -> p j o", o=1).broadcast_to([128, NJ, 4])),
        waits=[(Sy, LDC)])
    e(V, lambda v: v.memset(Sst[:], 0.0))
    e(V, lambda v: v.memset(carry[:], 0.0))
    e(V, lambda v: v.memset(gmax1[:], 0.0))
    e(V, lambda v: v.memset(gmax2[:], 0.0))

    try:
     if upto == "consts":
        raise _Cut()
     with contextlib.ExitStack() as sA:
        def sb(n, s, d):
            return sA.enter_context(nc.sbuf_tensor(n, s, d))

        xld = sb("a_xld", [128, 2 * D], F32)
        xnbf = sb("a_xnbf", [128, D], BF16)
        lnx_t = sb("a_lnxT", [128, 4 * D], BF16)
        tail3 = sb("a_tail3", [128, NJ * 3], BF16)
        gate_t = sb("a_gate", [128, NCHS * D], BF16)
        azero = sb("a_azero", [128, ST], F32)
        yd_st = sb("a_ydst", [128, NJ * ST], BF16)
        wsA = sb("a_wsA", [128, 2 * NJ * 128], BF16)
        wsB = sb("a_wsB", [128, 2 * NJ * 128], BF16)
        wsC = sb("a_wsC", [128, 2 * NJ * 128], BF16)
        wrow = sb("a_wrow", [128, 2 * D], BF16)

        vz = e(V, lambda v: v.memset(tail3[:], 0.0))
        vz = e(V, lambda v: v.memset(azero[:], 0.0))

        xld_guard = [0, 0]

        def wslice(g, wsrc, buf, par, j):
            return g.dma_start(
                out=buf[:, par * NJ * 128:(par + 1) * NJ * 128].rearrange("p (i c) -> p i c", i=NJ),
                in_=wsrc.rearrange("(i p) o -> p i o", p=128)[:, :, j * 128:(j + 1) * 128])

        def _emit_st(st):
            s1 = contextlib.ExitStack()
            sb1 = lambda n, s, d, st=st: s1.enter_context(nc.sbuf_tensor(f"st{st}_" + n, s, d))
            hrms_t = sb1("hrmsT", [128, D], BF16)
            hrms_d = sb1("hrmsD", [128, NJ * (ST + 3)], BF16)
            lnx_d = sb1("lnxD", [128, NJ * ST], BF16)
            h_d = sb1("hD", [128, NJ * ST], BF16)
            qn = sb1("qn", [128, NJ * ST], BF16)
            kn = sb1("kn", [128, NJ * ST], BF16)
            kbet = sb1("kbeta", [128, NJ * ST], BF16)
            lnal = sb1("lnal", [128, 2 * ST], F32)
            cst = sb1("cst", [128, NJ * (ST + 1)], F32)
            vpr = sb1("vpr", [128, NCHS * D], BF16)
            silu1 = sb1("silu1", [64, ST], BF16)
            beta_d = sb1("betaD", [16, ST], BF16)
            beta_t = sb1("betaT", [128, NCHS * 16], BF16)
            sqs = sb1("sqs", [128, 2 * ST], BF16)
            normb = sb1("normb", [16, ST], F32)
            ecb = sb1("ecb", [128, NJ * (C + 1)], F32)
            emcb = sb1("emcb", [128, NJ * (C + 1)], F32)
            mpos = sb1("mpos", [128, NJ], F32)
            mneg = sb1("mneg", [128, NJ], F32)
            bCs = sb1("bCs", [128, NJ], F32)
            kapb = sb1("kapb", [128, NJ * C], BF16)
            kprc = sb1("kprc", [128, NJ * C], BF16)
            kpad = sb1("kpad", [128, NJ * C], BF16)
            qcc = sb1("qcc", [128, NJ * C], BF16)
            kccb = sb1("kcc", [128, NJ * C], BF16)
            kcct = sb1("kcct", [128, NJ * C], BF16)
            s0p = sb1("s0p", [128, NJ * 64], BF16)
            powN = sb1("powN", [128, 2 * 128], BF16)
            powT = sb1("powT", [128, 2 * 128], BF16)
            pl1t = sb1("pl1t", [128, 2 * 128], BF16)
            lqts = sb1("lqt", [128, 2 * 128], BF16)
            xb = sb1("xb", [128, 2 * 64], BF16)
            ystg = sb1("ystg", [128, D], F32)
            ybf = sb1("ybf", [128, D], BF16)
            vz = e(V, lambda v: v.memset(pl1t[:], 0.0), waits=[(Tn, kb.ctr[Tn])])
            vz = e(V, lambda v: v.memset(kpad[:], 0.0))
            vz = e(V, lambda v: v.memset(lqts[:], 0.0))
            vz = e(V, lambda v: v.memset(powN[:], 0.0))
            vz = e(V, lambda v: v.memset(powT[:], 0.0))
            kb.region = "ln"
            for ti in range(4):
                gt = st * 4 + ti
                par = gt % 2
                xd = e(Sy, lambda y, g=gt, p=par: y.dma_start(
                    out=xld[:, p * D:(p + 1) * D], in_=x_h[g * 128:(g + 1) * 128, :]),
                    waits=[(V, xld_guard[par])])
                xsl = lambda p=par: xld[:, p * D:(p + 1) * D]
                vw = e(V, lambda v, p=par: v.reduce_sum(stat[:, 0:1], xld[:, p * D:(p + 1) * D], AX.X),
                       waits=[(Sy, xd), (Sc, kb.ctr[Sc]), (V, kb.ctr[V])])
                vw = e(V, lambda v: v.tensor_scalar_mul(stat[:, 1:2], stat[:, 0:1], -1.0 / D))
                sw = e(Sc, lambda s, p=par: s.activation(scr1[:], xld[:, p * D:(p + 1) * D], AF.Square,
                                                         bias=stat[:, 1:2], scale=1.0,
                                                         accum_out=stat[:, 2:3]),
                       waits=[(V, vw)])
                vw = e(V, lambda v: v.tensor_scalar(stat[:, 3:4], stat[:, 2:3], 1.0 / D, 1e-5,
                                                    ALU.mult, ALU.add), waits=[(Sc, sw)])
                vw = e(V, lambda v: v.reciprocal(stat[:, 4:5], stat[:, 3:4]))
                sw = e(Sc, lambda s: s.activation(stat[:, 5:6], stat[:, 4:5], AF.Sqrt), waits=[(V, vw)])
                vw = e(V, lambda v: v.tensor_scalar(stat[:, 6:7], stat[:, 2:3], 1.0 / D, 0.0,
                                                    ALU.mult, ALU.add), waits=[(Sc, sw)])
                vw = e(V, lambda v: v.tensor_mul(stat[:, 7:8], stat[:, 6:7], stat[:, 4:5]))
                vw = e(V, lambda v: v.tensor_scalar(stat[:, 7:8], stat[:, 7:8], 1.0, EPS, ALU.mult, ALU.add))
                vw = e(V, lambda v: v.reciprocal(stat[:, 8:9], stat[:, 7:8]))
                sw = e(Sc, lambda s: s.activation(stat[:, 9:10], stat[:, 8:9], AF.Sqrt), waits=[(V, vw)])
                vw = e(V, lambda v: v.tensor_mul(stat[:, 10:11], stat[:, 9:10], stat[:, 5:6]),
                       waits=[(Sc, sw)])
                vw = e(V, lambda v, ti=ti, p=par: v.tensor_scalar(
                    lnx_t[:, ti * D:(ti + 1) * D], xld[:, p * D:(p + 1) * D],
                    stat[:, 1:2], stat[:, 5:6], ALU.add, ALU.mult), waits=[(V, vw)])
                vw = e(V, lambda v, p=par: v.tensor_scalar(
                    hrms_t[:, 0:D], xld[:, p * D:(p + 1) * D],
                    stat[:, 1:2], stat[:, 10:11], ALU.add, ALU.mult))
                xld_guard[par] = vw
                for jg in range(2):
                    tws = 0
                    for jj in range(4):
                        j = jg * 4 + jj
                        jw = bankw(7, [(V, vw)]) if jj == 0 else ()
                        tws = e(Tn, lambda t, j=j, ti=ti, q=jj: t.transpose(
                            PT[:, q * 256:q * 256 + 128],
                            lnx_t[:, ti * D + j * 128:ti * D + j * 128 + 128], ident[:]),
                            waits=jw)
                        tws = e(Tn, lambda t, j=j, q=jj: t.transpose(
                            PT[:, q * 256 + 128:q * 256 + 256],
                            hrms_t[:, j * 128:j * 128 + 128], ident[:]))
                    vv = e(V, lambda v, jg=jg, ti=ti: v.tensor_copy(
                        lnx_d[:].rearrange("p (j t) -> p j t", j=NJ)[
                            :, jg * 4:jg * 4 + 4, ti * 128:ti * 128 + 128],
                        PT[:, 0:1024].rearrange("p (q x) -> p q x", q=4)[:, :, 0:128]),
                        waits=[(Tn, tws)], nsw=True)
                    vv = e(V, lambda v, jg=jg, ti=ti: v.tensor_copy(
                        hrms_d[:].rearrange("p (j c) -> p j c", j=NJ)[
                            :, jg * 4:jg * 4 + 4, 3 + ti * 128:3 + ti * 128 + 128],
                        PT[:, 0:1024].rearrange("p (q x) -> p q x", q=4)[:, :, 128:256]),
                        nsw=True)
                    bank_done(7, V)
            vv = e(V, lambda v: v.tensor_copy(
                hrms_d[:].rearrange("p (j c) -> p j c", j=NJ)[:, :, 0:3],
                tail3[:].rearrange("p (j c) -> p j c", j=NJ)))
            for j in range(NJ):
                base = j * (ST + 3)
                vv = e(V, lambda v, b=base, j=j: v.tensor_scalar(
                    scr2[:, 0:ST], hrms_d[:, b:b + ST], convw[:, 4 * j:4 * j + 1], None, ALU.mult),
                    waits=[(Sc, kb.ctr[Sc])])
                for kk in range(1, 4):
                    vv = e(V, lambda v, b=base, j=j, kk=kk: v.scalar_tensor_tensor(
                        scr2[:, 0:ST], hrms_d[:, b + kk:b + kk + ST],
                        convw[:, 4 * j + kk:4 * j + kk + 1], scr2[:, 0:ST], ALU.mult, ALU.add))
                sch = e(Sc, lambda s, j=j: s.activation(h_d[:, j * ST:(j + 1) * ST], scr2[:, 0:ST],
                                                        AF.Silu, bias=convb[:, j:j + 1], scale=1.0),
                        waits=[(V, vv)])
            SC_H = kb.ctr[Sc]
            vv = e(V, lambda v: v.tensor_copy(
                tail3[:].rearrange("p (j c) -> p j c", j=NJ),
                hrms_d[:].rearrange("p (j c) -> p j c", j=NJ)[:, :, ST:ST + 3]),
                waits=[(Sc, SC_H)])

            if upto == "a1":
                raise _Cut()
            kb.region = "proj"
            gmk = {}
            g0 = e(Gp, lambda g: wslice(g, w_q, wsA, 0, 0), waits=[(Tn, kb.ctr[Tn])])
            g0 = e(Gp, lambda g: wslice(g, w_k, wsB, 0, 0))
            g0 = e(Gp, lambda g: wslice(g, w_a, wsC, 0, 0))
            gmk[0] = g0
            last_mm_par = [0, 0]
            for j in range(NJ):
                par = j % 2
                if j + 1 < NJ:
                    gn = e(Gp, lambda g, j=j: wslice(g, w_q, wsA, (j + 1) % 2, j + 1),
                           waits=[(Tn, last_mm_par[(j + 1) % 2])])
                    gn = e(Gp, lambda g, j=j: wslice(g, w_k, wsB, (j + 1) % 2, j + 1))
                    gn = e(Gp, lambda g, j=j: wslice(g, w_a, wsC, (j + 1) % 2, j + 1))
                    gmk[j + 1] = gn
                tq = tk = ta = 0
                for i in range(NJ):
                    tq = e(Tn, lambda t, i=i, p=par: t.matmul(
                        P[0][:, 0:512], wsA[:, p * 1024 + i * 128:p * 1024 + i * 128 + 128],
                        h_d[:, i * ST:i * ST + 512], start=(i == 0), stop=(i == NJ - 1)),
                        waits=(bankw(0, [(Gp, gmk[j]), (Sc, SC_H)]) if i == 0 else ()))
                vq = e(V, lambda v, j=j: v.tensor_copy(qn[:, j * ST:(j + 1) * ST], P[0][:, 0:512]),
                       waits=[(Tn, tq)], nsw=True)
                bank_done(0, V)
                for i in range(NJ):
                    tk = e(Tn, lambda t, i=i, p=par: t.matmul(
                        P[1][:, 0:512], wsB[:, p * 1024 + i * 128:p * 1024 + i * 128 + 128],
                        h_d[:, i * ST:i * ST + 512], start=(i == 0), stop=(i == NJ - 1)),
                        waits=(bankw(1) if i == 0 else ()))
                vk = e(V, lambda v, j=j: v.tensor_copy(kn[:, j * ST:(j + 1) * ST], P[1][:, 0:512]),
                       waits=[(Tn, tk)], nsw=True)
                bank_done(1, V)
                for i in range(NJ):
                    ta = e(Tn, lambda t, i=i, p=par: t.matmul(
                        P[2][:, 0:512], wsC[:, p * 1024 + i * 128:p * 1024 + i * 128 + 128],
                        h_d[:, i * ST:i * ST + 512], start=(i == 0), stop=(i == NJ - 1)),
                        waits=(bankw(2) if i == 0 else ()))
                lp = j % 2
                sa1 = e(Sc, lambda s, j=j, lp=lp: s.activation(
                    lnal[:, lp * ST:lp * ST + ST], P[2][:, 0:512], AF.Sigmoid,
                    bias=alphab[:, j:j + 1], scale=1.0),
                    waits=[(Tn, ta), (V, kb.ctr[V])])
                sa1 = e(Sc, lambda s, lp=lp: s.activation(
                    lnal[:, lp * ST:lp * ST + ST], lnal[:, lp * ST:lp * ST + ST], AF.Ln))
                bank_done(2, Sc)
                cb = j * (ST + 1)
                vs_ = e(V, lambda v, cb=cb, j=j: v.tensor_copy(cst[:, cb:cb + 1], carry[:, j:j + 1]),
                        waits=[(Sc, sa1), (V, kb.ctr[V])])
                vs_ = e(V, lambda v, cb=cb, lp=lp: v.tensor_tensor_scan(
                    cst[:, cb + 1:cb + 1 + ST], lnal[:, lp * ST:lp * ST + ST],
                    azero[:, 0:ST], cst[:, cb:cb + 1], ALU.add, ALU.add),
                    waits=[(V, vs_)])
                vs_ = e(V, lambda v, cb=cb, j=j: v.tensor_copy(carry[:, j:j + 1],
                                                               cst[:, cb + ST:cb + ST + 1]),
                        waits=[(V, vs_)])
                last_mm_par[par] = ta
            for buf, tens in [(0, qn), (1, kn)]:
                tpv = 0
                for j in range(NJ):
                    vsq = e(V, lambda v, j=j, b=buf, tens=tens: v.tensor_mul(
                        sqs[:, b * ST:(b + 1) * ST], tens[:, j * ST:(j + 1) * ST],
                        tens[:, j * ST:(j + 1) * ST]), waits=[(Tn, tpv), (V, kb.ctr[V])],
                        nsw=True)
                    tpv = e(Tn, lambda t, j=j, b=buf: t.matmul(
                        P[5][0:16, 0:512], Fmat[:, j * 16:(j + 1) * 16], sqs[:, b * ST:(b + 1) * ST],
                        start=(j == 0), stop=(j == NJ - 1)),
                        waits=(bankw(5, [(V, vsq)]) if j == 0 else [(V, vsq)]))
                sn = e(Sc, lambda s: s.activation(normb[0:16, 0:512], P[5][0:16, 0:512], AF.Sqrt),
                    waits=[(Tn, tpv), (V, kb.ctr[V])])
                bank_done(5, Sc)
                vn = e(V, lambda v: v.tensor_scalar(normb[0:16, 0:512], normb[0:16, 0:512], 1.0, EPS,
                                                    ALU.mult, ALU.add), waits=[(Sc, sn)])
                vn = e(V, lambda v: v.reciprocal(normb[0:16, 0:512], normb[0:16, 0:512]))
                vn = e(V, lambda v, b=buf: v.tensor_copy(sqs[0:16, b * ST:b * ST + 512],
                                                         normb[0:16, 0:512]), waits=[(V, vn)])
                for j in range(NJ):
                    te2 = e(Tn, lambda t, j=j, b=buf: t.matmul(
                        P[4][:, 0:512], Emat[:, j * 128:(j + 1) * 128],
                        sqs[0:16, b * ST:b * ST + 512], start=True, stop=True),
                        waits=bankw(4, [(V, vn)]))
                    e(V, lambda v, j=j, tens=tens: v.tensor_mul(tens[:, j * ST:(j + 1) * ST],
                                                     tens[:, j * ST:(j + 1) * ST], P[4][:, 0:512]),
                      waits=[(Tn, te2)], nsw=True)
                    bank_done(4, V)
            VQK = kb.ctr[V]
            if st == 0:
                dump("qinv", lambda: sqs[0:16, 0:512], [16, 512], BF16)
                dump("zq_post", lambda: qn[:, 0:NJ * ST], [128, NJ * ST], BF16)
            if upto == "qk":
                raise _Cut()
            tb = 0
            for i in range(NJ):
                tb = e(Tn, lambda t, i=i: t.matmul(
                    P[5][0:16, 0:512], betaw[:, i * 16:(i + 1) * 16], h_d[:, i * ST:i * ST + 512],
                    start=(i == 0), stop=(i == NJ - 1)),
                    waits=(bankw(5, [(Gp, LDG)]) if i == 0 else ()))
            sbeta = e(Sc, lambda s: s.activation(beta_d[:, 0:ST], P[5][0:16, 0:512], AF.Sigmoid,
                                               bias=betab[0:16, 0:1], scale=1.0), waits=[(Tn, tb)])
            bank_done(5, Sc)
            for ccw in range(NCHS):
                tb2 = e(Tn, lambda t, cc=ccw: t.transpose(
                    PT[:, cc * 16:cc * 16 + 16],
                    beta_d[0:16, cc * 128:(cc + 1) * 128], ident[0:16, 0:16]),
                    waits=(bankw(7, [(Sc, sbeta)]) if ccw == 0 else ()))
            vbt = e(V, lambda v: v.tensor_copy(beta_t[:, 0:NCHS * 16], PT[:, 0:NCHS * 16]),
                    waits=[(Tn, kb.ctr[Tn])])
            bank_done(7, V)
            for j in range(NJ):
                te2 = e(Tn, lambda t, j=j: t.matmul(
                    P[4][:, 0:512], Emat[:, j * 128:(j + 1) * 128], beta_d[0:16, 0:512],
                    start=True, stop=True), waits=bankw(4, [(Sc, sbeta), (V, VQK)]))
                e(V, lambda v, j=j: v.tensor_mul(kbet[:, j * ST:(j + 1) * ST],
                                                 kn[:, j * ST:(j + 1) * ST], P[4][:, 0:512]),
                  waits=[(Tn, te2)])
                bank_done(4, V)
            wv_guard = [0, 0]
            for hf in range(2):
                gv = 0
                tvcc = [0, 0, 0, 0]
                for i in range(NJ):
                    par = i % 2
                    gv = e(Gp, lambda g, i=i, h=hf, p=par: g.dma_start(
                        out=wrow[:, p * D:p * D + 512],
                        in_=w_v[i * 128:(i + 1) * 128, h * 512:h * 512 + 512]),
                        waits=[(Tn, wv_guard[par])] + (
                            [(Tn, last_mm_par[0]), (Tn, last_mm_par[1]), (V, VQK)] if i == 0 and hf == 0 else []))
                    for ccv in range(NCHS):
                        w0 = []
                        if i == 0:
                            w0 = bankw(ccv, [(Gp, gv)])
                        tvcc[ccv] = e(Tn, lambda t, i=i, cc=ccv, p=par: t.matmul(
                            P[cc][:, 0:512], h_d[:, i * ST + cc * 128:i * ST + cc * 128 + 128],
                            wrow[:, p * D:p * D + 512],
                            start=(i == 0), stop=(i == NJ - 1)),
                            waits=(w0 + ([(Gp, gv)] if i > 0 and ccv == 0 else [])))
                    wv_guard[par] = tvcc[NCHS - 1]
                for ccv in range(NCHS):
                    e(V, lambda v, cc=ccv, h=hf: v.tensor_mul(
                        vpr[:, cc * D + h * 512:cc * D + h * 512 + 512].rearrange(
                            "p (hh vv) -> p hh vv", vv=64),
                        P[cc][:, 0:512].rearrange("p (hh vv) -> p hh vv", vv=64),
                        beta_t[:, cc * 16 + h * 8:cc * 16 + h * 8 + 8].rearrange(
                            "p (hh o) -> p hh o", o=1).broadcast_to([128, 8, 64])),
                      waits=[(Tn, tvcc[ccv]), (V, vbt)])
                    bank_done(ccv, V)
            tu = 0
            for i in range(NJ):
                tu = e(Tn, lambda t, i=i: t.matmul(
                    P[2][0:64, 0:512], u1w_s[:, i * 64:(i + 1) * 64], lnx_d[:, i * ST:i * ST + 512],
                    start=(i == 0), stop=(i == NJ - 1)),
                    waits=(bankw(2, [(Gp, LDG)]) if i == 0 else ()))
            su = e(Sc, lambda s: s.activation(silu1[:, 0:ST], P[2][0:64, 0:512], AF.Silu,
                                              bias=u1b_s[0:64, 0:1], scale=1.0), waits=[(Tn, tu)])
            bank_done(2, Sc)
            for ccx in range(NCHS):
                for hf in range(2):
                    tg = e(Tn, lambda t, cc=ccx, h=hf: t.matmul(
                        P[2][:, 0:512], silu1[0:64, cc * 128:(cc + 1) * 128],
                        u2w_s[0:64, h * 512:h * 512 + 512], start=True, stop=True),
                        waits=bankw(2, [(Sc, su)]))
                    vg = e(V, lambda v, h=hf: v.tensor_add(scr1[:, 0:512], P[2][:, 0:512],
                                                           u2bb[:, h * 512:h * 512 + 512]),
                           waits=[(Tn, tg), (Sc, kb.ctr[Sc])])
                    bank_done(2, V)
                    e(Sc, lambda s, cc=ccx, h=hf: s.activation(
                        gate_t[:, cc * D + h * 512:cc * D + h * 512 + 512], scr1[:, 0:512],
                        AF.Sigmoid), waits=[(V, vg)])
            SC_GATE = kb.ctr[Sc]

            if st == 0:
                dump("lnx_t", lambda: lnx_t[:, 0:4 * D], [128, 4 * D], BF16)
                dump("h_d", lambda: h_d[:], [128, NJ * ST], BF16)
                dump("qn", lambda: qn[:], [128, NJ * ST], BF16)
                dump("kn", lambda: kn[:], [128, NJ * ST], BF16)
                dump("kbet", lambda: kbet[:], [128, NJ * ST], BF16)
                dump("cst", lambda: cst[:], [128, NJ * (ST + 1)], F32)
                dump("vpr", lambda: vpr[:], [128, NCHS * D], BF16)
                dump("gate_t", lambda: gate_t[:], [128, NCHS * D], BF16)
            if upto == "a2":
                raise _Cut()
            kb.region = "scan"
            for ccx in range(NCHS):
                vm = 0
                for j in range(NJ):
                    cb = j * (ST + 1) + ccx * 128
                    vm = e(V, lambda v, j=j, cb=cb: v.tensor_scalar(
                        mpos[:, j:j + 1], cst[:, cb:cb + 1], cst[:, cb + 128:cb + 129],
                        0.5, ALU.add, ALU.mult),
                        waits=([(V, kb.ctr[V]), (Sc, kb.ctr[Sc]), (Tn, kb.ctr[Tn]),
                                ("out", kb.ctr["out"])] if j == 0 else ()))
                    vm = e(V, lambda v, j=j, cb=cb: v.tensor_scalar(
                        mneg[:, j:j + 1], cst[:, cb:cb + 1], cst[:, cb + 128:cb + 129],
                        -0.5, ALU.add, ALU.mult))
                se2 = 0
                for j in range(NJ):
                    cb = j * (ST + 1) + ccx * 128
                    eb = j * (C + 1)
                    se2 = e(Sc, lambda s, cb=cb, eb=eb, j=j: s.activation(
                        ecb[:, eb:eb + C + 1], cst[:, cb:cb + C + 1], AF.Exp,
                        bias=mneg[:, j:j + 1], scale=1.0),
                        waits=([(V, vm)] if j == 0 else ()))
                    se2 = e(Sc, lambda s, cb=cb, eb=eb, j=j: s.activation(
                        emcb[:, eb:eb + C + 1], cst[:, cb:cb + C + 1], AF.Exp,
                        bias=mpos[:, j:j + 1], scale=-1.0))
                kn3 = kn[:].rearrange("p (j t) -> p j t", j=NJ)[:, :, ccx * 128:(ccx + 1) * 128]
                kb3 = kbet[:].rearrange("p (j t) -> p j t", j=NJ)[:, :, ccx * 128:(ccx + 1) * 128]
                qn3 = qn[:].rearrange("p (j t) -> p j t", j=NJ)[:, :, ccx * 128:(ccx + 1) * 128]
                ec3 = ecb[:].rearrange("p (j t) -> p j t", j=NJ)[:, :, 1:C + 1]
                em3 = emcb[:].rearrange("p (j t) -> p j t", j=NJ)[:, :, 1:C + 1]
                vx = e(V, lambda v, a=kn3, b=em3: v.tensor_mul(
                    kapb[:].rearrange("p (j t) -> p j t", j=NJ), a, b),
                    waits=[(Sc, se2), (V, kb.ctr[V]), (Tn, kb.ctr[Tn])])
                vx = e(V, lambda v, a=kb3, b=ec3: v.tensor_mul(
                    kprc[:].rearrange("p (j t) -> p j t", j=NJ), a, b), nsw=True)
                vx = e(V, lambda v, a=qn3, b=ec3: v.tensor_mul(
                    qcc[:].rearrange("p (j t) -> p j t", j=NJ), a, b), nsw=True)
                vx = e(V, lambda v: v.tensor_copy(
                    kpad[:].rearrange("p (j t) -> p j t", j=NJ)[:, :, 64:128],
                    kprc[:].rearrange("p (j t) -> p j t", j=NJ)[:, :, 64:128]), nsw=True)
                for j in range(NJ):
                    eb = j * (C + 1)
                    vx = e(V, lambda v, j=j, eb=eb: v.tensor_scalar(
                        kccb[:, j * C:(j + 1) * C], kapb[:, j * C:(j + 1) * C],
                        ecb[:, eb + C:eb + C + 1], None, ALU.mult))
                    vx = e(V, lambda v, j=j, eb=eb: v.tensor_mul(
                        bCs[:, j:j + 1], ecb[:, eb + C:eb + C + 1], emcb[:, eb:eb + 1]))
                    vx = e(V, lambda v, j=j, eb=eb: v.tensor_scalar(
                        s0p[:, j * 64:(j + 1) * 64], Sst[:, j * 64:(j + 1) * 64],
                        emcb[:, eb:eb + 1], None, ALU.mult))
                VPREP = kb.ctr[V]
                if st == 0 and ccx == 0:
                    dump("ecb", lambda: ecb[:], [128, NJ * (C + 1)], F32)
                    dump("emcb", lambda: emcb[:], [128, NJ * (C + 1)], F32)
                    dump("kapb", lambda: kapb[:], [128, NJ * C], BF16)
                    dump("kprc", lambda: kprc[:], [128, NJ * C], BF16)
                    dump("qcc", lambda: qcc[:], [128, NJ * C], BF16)
                    dump("kccb", lambda: kccb[:], [128, NJ * C], BF16)
                    dump("s0p", lambda: s0p[:], [128, NJ * 64], BF16)
                    dump("bCs", lambda: bCs[:], [128, NJ], F32)
                tkt = 0
                for j in range(NJ):
                    tkt = e(Tn, lambda t, j=j: t.transpose(
                        PT[:, j * 128:(j + 1) * 128], kccb[:, j * C:(j + 1) * C], ident[:]),
                        waits=(bankw(7, [(V, VPREP)]) if j == 0 else ()))
                vkt = e(V, lambda v: v.tensor_copy(kcct[:, 0:512], PT[:, 0:512]), waits=[(Tn, tkt)],
                        nsw=True)
                vkt = e(V, lambda v: v.tensor_copy(kcct[:, 512:1024], PT[:, 512:1024]), nsw=True)
                bank_done(7, V)
                evac_par = [0, 0]
                state_ev = 0
                for hh in range(H):
                    j, hp, par = hh // 2, hh % 2, hh % 2
                    rs = slice(64 * hp, 64 * hp + 64)
                    kap_ = lambda lo, sz, j=j, rs=rs: kapb[rs, j * C + lo:j * C + lo + sz]
                    kpr_ = lambda lo, sz, j=j, rs=rs: kprc[rs, j * C + lo:j * C + lo + sz]
                    qc_ = lambda lo, sz, j=j, rs=rs: qcc[rs, j * C + lo:j * C + lo + sz]
                    pA = P[0 + par]
                    pB = P[2 + par]
                    pR = P[4 + par]
                    pS = P[6]
                    tm = e(Tn, lambda t, f=kap_, g2=kpr_, pA=pA: t.matmul(
                        pA[0:64, 0:64], f(0, 64), g2(0, 64), start=True, stop=True),
                        waits=[(V, VPREP), (V, evac_par[par]), (V, vkt),
                               ("out", kb.ctr["out"])])
                    tm = e(Tn, lambda t, f=kap_, g2=kpr_, pA=pA: t.matmul(
                        pA[:, 64:128], f(0, 128), g2(64, 64), start=True, stop=True))
                    tm = e(Tn, lambda t, f=kap_, g2=kpr_, pB=pB: t.matmul(
                        pB[:, 0:64], g2(0, 128), f(0, 64), start=True, stop=True))
                    tmf = e(Tn, lambda t, f=kap_, j=j, rs=rs, pB=pB: t.matmul(
                        pB[:, 64:128], kpad[rs, j * C:(j + 1) * C], f(64, 64),
                        start=True, stop=True))
                    vv1 = e(V, lambda v, pA=pA, par=par: v.copy_predicated(
                        pl1t[:, par * 128:(par + 1) * 128], masku[:], pA[:, 0:128]),
                        waits=[(Tn, tmf)])
                    vv1 = e(V, lambda v, pB=pB, par=par: v.copy_predicated(
                        powN[:, par * 128:(par + 1) * 128], maskl[:], pB[:, 0:128]))
                    tm = e(Tn, lambda t, f=kap_, q2=qc_, pA=pA: t.matmul(
                        pA[0:64, 128:192], f(0, 64), q2(0, 64), start=True, stop=True),
                        waits=[(V, vv1)])
                    tmq = e(Tn, lambda t, f=kap_, q2=qc_, pA=pA: t.matmul(
                        pA[:, 192:256], f(0, 128), q2(64, 64), start=True, stop=True))
                    vv2 = e(V, lambda v, pA=pA, par=par: v.copy_predicated(
                        lqts[:, par * 128:(par + 1) * 128], maskui[:], pA[:, 128:256]),
                        waits=[(Tn, tmq)])
                    tm = e(Tn, lambda t, g2=kpr_, j=j, hp=hp, pR=pR: t.matmul(
                        pR[:, 0:64], g2(0, 128), s0p[64 * hp:64 * hp + 64, j * 64:j * 64 + 64],
                        start=True, stop=True))
                    vv3 = e(V, lambda v, cc=ccx, hh=hh, par=par, pR=pR: v.tensor_sub(
                        xb[:, par * 64:(par + 1) * 64],
                        vpr[:, cc * D + hh * 64:cc * D + hh * 64 + 64], pR[:, 0:64]),
                        waits=[(Tn, tm)])
                    if st == 0 and ccx == 0 and hh == 0:
                        dump("plt0", lambda: pl1t[:, 0:128], [128, 128], BF16)
                        dump("pln0", lambda: powN[:, 0:128], [128, 128], BF16)
                        dump("lqt0", lambda: lqts[:, 0:128], [128, 128], BF16)
                        dump("rp0", lambda: xb[:, 0:64], [128, 64], BF16)
                    tm = e(Tn, lambda t, par=par, pR=pR: t.matmul(
                        pR[:, 64:128], pl1t[:, par * 128:(par + 1) * 128],
                        xb[:, par * 64:(par + 1) * 64], start=True, stop=True), waits=[(V, vv3)])
                    vap = e(V, lambda v, par=par, pR=pR: v.tensor_sub(
                        xb[:, par * 64:(par + 1) * 64], xb[:, par * 64:(par + 1) * 64],
                        pR[:, 64:128]), waits=[(Tn, tm)])
                    srcN, srcT = powN, pl1t
                    scop = kb.ctr[Sc]
                    lev = 2
                    while lev <= 64:
                        if lev < 64:
                            tmN = e(Tn, lambda t, a=srcT, b=srcN, par=par, pA=pA: t.matmul(
                                pA[:, 256:384], a[:, par * 128:(par + 1) * 128],
                                b[:, par * 128:(par + 1) * 128], start=True, stop=True),
                                waits=[(Sc, scop), (V, vap)])
                        tmT = e(Tn, lambda t, a=srcN, b=srcT, par=par, pA=pA: t.matmul(
                            pA[:, 384:512], a[:, par * 128:(par + 1) * 128],
                            b[:, par * 128:(par + 1) * 128], start=True, stop=True),
                            waits=([(Sc, scop), (V, vap)] if lev == 64 else ()))
                        if lev < 64:
                            scop = e(Sc, lambda s, par=par, pA=pA: s.copy(
                                powN[:, par * 128:(par + 1) * 128], pA[:, 256:384]),
                                waits=[(Tn, tmT)])
                        scop = e(Sc, lambda s, par=par, pA=pA: s.copy(
                            powT[:, par * 128:(par + 1) * 128], pA[:, 384:512]),
                            waits=([(Tn, tmT)] if lev == 64 else ()))
                        tm = e(Tn, lambda t, par=par, pR=pR: t.matmul(
                            pR[:, 64:128], powT[:, par * 128:(par + 1) * 128],
                            xb[:, par * 64:(par + 1) * 64], start=True, stop=True),
                            waits=[(Sc, scop)])
                        vap = e(V, lambda v, par=par, pR=pR: v.tensor_add(
                            xb[:, par * 64:(par + 1) * 64], xb[:, par * 64:(par + 1) * 64],
                            pR[:, 64:128]), waits=[(Tn, tm)])
                        srcN, srcT = powN, powT
                        lev *= 2
                    if st == 0 and ccx == 0 and hh == 0:
                        dump("u0", lambda: xb[:, 0:64], [128, 64], BF16)
                    tm = e(Tn, lambda t, q2=qc_, j=j, hp=hp, pR=pR: t.matmul(
                        pR[:, 128:192], q2(0, 128),
                        s0p[64 * hp:64 * hp + 64, j * 64:j * 64 + 64], start=True, stop=False),
                        waits=[(V, vap)])
                    tm = e(Tn, lambda t, par=par, pR=pR: t.matmul(
                        pR[:, 128:192], lqts[:, par * 128:(par + 1) * 128],
                        xb[:, par * 64:(par + 1) * 64], start=False, stop=True))
                    sq1 = e(Sc, lambda s, hh=hh, pR=pR: s.activation(
                        scr1[:, (hh % 8) * 64:(hh % 8) * 64 + 64], pR[:, 128:192], AF.Square,
                        accum_out=stat[:, 11:12]), waits=[(Tn, tm), (V, kb.ctr[V])])
                    vr = e(V, lambda v: v.tensor_scalar(stat[:, 12:13], stat[:, 11:12],
                                                        1.0 / 64.0, EPS, ALU.mult, ALU.add),
                           waits=[(Sc, sq1), (V, kb.ctr[V])])
                    vr = e(V, lambda v: v.reciprocal(stat[:, 13:14], stat[:, 12:13]))
                    sr = e(Sc, lambda s: s.activation(stat[:, 14:15], stat[:, 13:14], AF.Sqrt),
                           waits=[(V, vr)])
                    vr = e(V, lambda v, hh=hh, pR=pR: v.scalar_tensor_tensor(
                        ystg[:, hh * 64:(hh + 1) * 64], pR[:, 128:192], stat[:, 14:15],
                        hnb[:, hh * 64:(hh + 1) * 64], ALU.mult, ALU.mult), waits=[(Sc, sr)])
                    tm = e(Tn, lambda t, j=j, par=par, pS=pS: t.matmul(
                        pS[:, 0:64], kcct[:, j * 128:(j + 1) * 128],
                        xb[:, par * 64:(par + 1) * 64], start=True, stop=True),
                        waits=[(V, state_ev), (V, vap)])
                    state_ev = e(V, lambda v, j=j, hp=hp, pS=pS: v.scalar_tensor_tensor(
                        Sst[64 * hp:64 * hp + 64, j * 64:j * 64 + 64],
                        Sst[64 * hp:64 * hp + 64, j * 64:j * 64 + 64],
                        bCs[64 * hp:64 * hp + 64, j:j + 1],
                        pS[64 * hp:64 * hp + 64, 0:64], ALU.mult, ALU.add),
                        waits=[(Tn, tm), (V, kb.ctr[V])])
                    evac_par[par] = max(vr, state_ev)
                THEADS = kb.ctr[Tn]
                if st == 0 and ccx == 0:
                    dump("ystg0", lambda: ystg[:], [128, D], F32)
                    dump("xb0", lambda: xb[:], [128, 2 * 64], BF16)
                    dump("S_c0", lambda: Sst[:], [128, NJ * 64], F32)
                vyb = e(V, lambda v: v.tensor_copy(ybf[:, 0:512], ystg[:, 0:512]),
                        waits=[(V, kb.ctr[V])])
                vyb = e(V, lambda v: v.tensor_copy(ybf[:, 512:1024], ystg[:, 512:1024]))
                tty = 0
                for j in range(NJ):
                    tty = e(Tn, lambda t, j=j: t.transpose(
                        PT[:, j * 128:(j + 1) * 128], ybf[:, j * 128:(j + 1) * 128], ident[:]),
                        waits=(bankw(7, [(V, vyb)]) if j == 0 else ()))
                vyd = e(V, lambda v, cc=ccx: v.tensor_copy(
                    yd_st[:].rearrange("p (j t) -> p j t", j=NJ)[:, :, cc * 128:(cc + 1) * 128],
                    PT[:, 0:1024].rearrange("p (j t) -> p j t", j=NJ)), waits=[(Tn, tty)], nsw=True)
                bank_done(7, V)
            VSCAN = kb.ctr[V]

            if st == 0:
                dump("yd_st", lambda: yd_st[:], [128, NJ * ST], BF16)
                dump("Sst", lambda: Sst[:], [128, NJ * 64], F32)
            s1.close()
            s2 = contextlib.ExitStack()
            sb2 = lambda n, s, d, st=st: s2.enter_context(nc.sbuf_tensor(f"st{st}b_" + n, s, d))
            yattn = sb2("yattn", [128, NCHS * D], F32)
            y2d = sb2("y2d", [128, NJ * ST], BF16)
            prodd = sb2("prod", [128, 32 * ST], BF16)
            ybf2 = sb2("ybf2", [128, D], BF16)
            if upto == "a3":
                raise _Cut()
            kb.region = "ffn"
            SC_SCAN = kb.ctr[Sc]
            wv_guard = [0, 0]
            for hf in range(2):
                gv = 0
                tvcc = [0, 0, 0, 0]
                for i in range(NJ):
                    par = i % 2
                    gv = e(Gp, lambda g, i=i, h=hf, p=par: g.dma_start(
                        out=wrow[:, p * D:p * D + 512],
                        in_=w_o[i * 128:(i + 1) * 128, h * 512:h * 512 + 512]),
                        waits=[(Tn, wv_guard[par])] + ([(Tn, kb.ctr[Tn])] if i == 0 and hf == 0 else []))
                    for ccv in range(NCHS):
                        w0 = []
                        if i == 0:
                            w0 = bankw(ccv, [(Gp, gv), (V, VSCAN), (Sc, SC_SCAN)])
                        elif ccv == 0:
                            w0 = [(Gp, gv)]
                        tvcc[ccv] = e(Tn, lambda t, i=i, cc=ccv, p=par: t.matmul(
                            P[cc][:, 0:512], yd_st[:, i * ST + cc * 128:i * ST + cc * 128 + 128],
                            wrow[:, p * D:p * D + 512],
                            start=(i == 0), stop=(i == NJ - 1)), waits=w0)
                    wv_guard[par] = tvcc[NCHS - 1]
                for ccv in range(NCHS):
                    vgp = e(V, lambda v, cc=ccv, h=hf: v.tensor_mul(
                        scr1[:, 0:512], P[cc][:, 0:512],
                        gate_t[:, cc * D + h * 512:cc * D + h * 512 + 512]),
                        waits=[(Tn, tvcc[ccv]), (Sc, SC_GATE)], nsw=True)
                    bank_done(ccv, V)
                    e(V, lambda v, cc=ccv, h=hf: v.tensor_add(
                        yattn[:, cc * D + h * 512:cc * D + h * 512 + 512], scr1[:, 0:512],
                        lnx_t[:, cc * D + h * 512:cc * D + h * 512 + 512]), nsw=True)
            for ccx in range(NCHS):
                vy2 = e(V, lambda v, cc=ccx: v.tensor_mul(scr2[:, 0:D], yattn[:, cc * D:(cc + 1) * D],
                                                          yattn[:, cc * D:(cc + 1) * D]),
                        waits=[(Sc, kb.ctr[Sc])])
                vy2 = e(V, lambda v: v.reduce_sum(stat[:, 11:12], scr2[:, 0:D], AX.X))
                vy2 = e(V, lambda v: v.tensor_scalar(stat[:, 12:13], stat[:, 11:12], 1.0 / D, EPS,
                                                     ALU.mult, ALU.add))
                vy2 = e(V, lambda v: v.reciprocal(stat[:, 13:14], stat[:, 12:13]))
                sy2 = e(Sc, lambda s: s.activation(stat[:, 14:15], stat[:, 13:14], AF.Sqrt),
                        waits=[(V, vy2)])
                vy2 = e(V, lambda v, cc=ccx: v.scalar_tensor_tensor(
                    ybf2[:, 0:D], yattn[:, cc * D:(cc + 1) * D], stat[:, 14:15], ffnb[:, 0:D],
                    ALU.mult, ALU.mult), waits=[(Sc, sy2), (V, kb.ctr[V])])
                tt2 = 0
                for j in range(NJ):
                    tt2 = e(Tn, lambda t, j=j: t.transpose(
                        PT[:, j * 128:(j + 1) * 128], ybf2[:, j * 128:(j + 1) * 128], ident[:]),
                        waits=(bankw(7, [(V, vy2)]) if j == 0 else ()))
                vy2 = e(V, lambda v, cc=ccx: v.tensor_copy(
                    y2d[:].rearrange("p (j t) -> p j t", j=NJ)[:, :, cc * 128:(cc + 1) * 128],
                    PT[:, 0:1024].rearrange("p (j t) -> p j t", j=NJ)), waits=[(Tn, tt2)], nsw=True)
                bank_done(7, V)
            VY2 = kb.ctr[V]
            if st == 0:
                dump("yattn", lambda: yattn[:], [128, NCHS * D], F32)
                dump("y2d", lambda: y2d[:], [128, NJ * ST], BF16)
            gf = e(Gp, lambda g: wslice(g, fw1, wsA, 0, 0), waits=[(Tn, kb.ctr[Tn])])
            gf = e(Gp, lambda g: wslice(g, fw3, wsB, 0, 0))
            gfm = {0: gf}
            lastf = [0, 0]
            vpp = 0
            for fj in range(32):
                par = fj % 2
                if fj + 1 < 32:
                    gf = e(Gp, lambda g, fj=fj: wslice(g, fw1, wsA, (fj + 1) % 2, fj + 1),
                           waits=[(Tn, lastf[(fj + 1) % 2])])
                    gf = e(Gp, lambda g, fj=fj: wslice(g, fw3, wsB, (fj + 1) % 2, fj + 1))
                    gfm[fj + 1] = gf
                t1 = 0
                for i in range(NJ):
                    t1 = e(Tn, lambda t, i=i, p=par: t.matmul(
                        P[0][:, 0:512], wsA[:, p * 1024 + i * 128:p * 1024 + i * 128 + 128],
                        y2d[:, i * ST:i * ST + 512], start=(i == 0), stop=(i == NJ - 1)),
                        waits=(bankw(0, [(Gp, gfm[fj]), (V, VY2)]) if i == 0 else ()))
                t3 = 0
                for i in range(NJ):
                    t3 = e(Tn, lambda t, i=i, p=par: t.matmul(
                        P[1][:, 0:512], wsB[:, p * 1024 + i * 128:p * 1024 + i * 128 + 128],
                        y2d[:, i * ST:i * ST + 512], start=(i == 0), stop=(i == NJ - 1)),
                        waits=(bankw(1) if i == 0 else ()))
                sfs = e(Sc, lambda s: s.activation(scr1[:, 0:512], P[0][:, 0:512], AF.Silu),
                        waits=[(Tn, t1), (V, vpp)])
                bank_done(0, Sc)
                vpp = e(V, lambda v, fj=fj: v.tensor_mul(prodd[:, fj * ST:(fj + 1) * ST],
                                                         scr1[:, 0:512], P[1][:, 0:512]),
                        waits=[(Sc, sfs), (Tn, t3)], nsw=True)
                bank_done(1, V)
                lastf[par] = t3
            VPROD = kb.ctr[V]
            wv_guard = [0, 0]
            for hf in range(2):
                gv = 0
                tvcc = [0, 0, 0, 0]
                for fj in range(32):
                    par = fj % 2
                    gv = e(Gp, lambda g, fj=fj, h=hf, p=par: g.dma_start(
                        out=wrow[:, p * D:p * D + 512],
                        in_=fw2[fj * 128:(fj + 1) * 128, h * 512:h * 512 + 512]),
                        waits=[(Tn, wv_guard[par])] + ([(Tn, kb.ctr[Tn])] if fj == 0 and hf == 0 else []))
                    for ccv in range(NCHS):
                        w0 = []
                        if fj == 0:
                            w0 = bankw(ccv, [(Gp, gv), (V, VPROD), (Sc, kb.ctr[Sc])])
                        elif ccv == 0:
                            w0 = [(Gp, gv)]
                        tvcc[ccv] = e(Tn, lambda t, fj=fj, cc=ccv, p=par: t.matmul(
                            P[cc][:, 0:512], prodd[:, fj * ST + cc * 128:fj * ST + cc * 128 + 128],
                            wrow[:, p * D:p * D + 512],
                            start=(fj == 0), stop=(fj == 31)), waits=w0)
                    wv_guard[par] = tvcc[NCHS - 1]
                for ccv in range(NCHS):
                    e(V, lambda v, cc=ccv, h=hf: v.tensor_add(
                        yattn[:, cc * D + h * 512:cc * D + h * 512 + 512],
                        yattn[:, cc * D + h * 512:cc * D + h * 512 + 512], P[cc][:, 0:512]),
                      waits=[(Tn, tvcc[ccv])], nsw=True)
                    bank_done(ccv, V)
            for ccx in range(NCHS):
                gt = st * 4 + ccx
                par = gt % 2
                xd = e(Sy, lambda y, g=gt, p=par: y.dma_start(
                    out=xld[:, p * D:(p + 1) * D], in_=x_h[g * 128:(g + 1) * 128, :]),
                    waits=[(V, xld_guard[par])])
                vq1 = e(V, lambda v, cc=ccx, p=par: v.tensor_add(
                    scr2[:, 0:D], yattn[:, cc * D:(cc + 1) * D], xld[:, p * D:(p + 1) * D]),
                    waits=[(Sy, xd), (Sc, kb.ctr[Sc]), (V, kb.ctr[V])])
                vq1 = e(V, lambda v: v.tensor_copy(xnbf[:, 0:D], scr2[:, 0:D]),
                        waits=[(Sy, kb.ctr[Sy])])
                e(Sy, lambda y, gt=gt: y.dma_start(out=xnew_h[gt * 128:(gt + 1) * 128, :],
                                                   in_=xnbf[:, 0:D]), waits=[(V, vq1)])
                xld_guard[par] = vq1
                vq1 = e(V, lambda v: v.reduce_sum(stat[:, 0:1], scr2[:, 0:D], AX.X))
                vq1 = e(V, lambda v, gt=gt: v.tensor_scalar_mul(m2s[:, gt:gt + 1], stat[:, 0:1],
                                                                -1.0 / D))
                sq2 = e(Sc, lambda s, gt=gt: s.activation(scr2[:, 0:D], scr2[:, 0:D], AF.Square,
                                                          bias=m2s[:, gt:gt + 1], scale=1.0,
                                                          accum_out=stat[:, 2:3]),
                        waits=[(V, vq1)])
                vq1 = e(V, lambda v: v.tensor_scalar(stat[:, 3:4], stat[:, 2:3], 1.0 / D, 1e-5,
                                                     ALU.mult, ALU.add), waits=[(Sc, sq2)])
                vq1 = e(V, lambda v: v.reciprocal(stat[:, 4:5], stat[:, 3:4]))
                sq2 = e(Sc, lambda s, gt=gt: s.activation(i2s[:, gt:gt + 1], stat[:, 4:5], AF.Sqrt),
                        waits=[(V, vq1)])
                vq1 = e(V, lambda v: v.reduce_max(stat[:, 5:6], scr2[:, 0:D], AX.X),
                        waits=[(Sc, sq2)])
                sq2 = e(Sc, lambda s: s.activation(stat[:, 6:7], stat[:, 5:6], AF.Sqrt),
                        waits=[(V, vq1)])
                vq1 = e(V, lambda v, gt=gt: v.tensor_mul(stat[:, 7:8], stat[:, 6:7],
                                                         i2s[:, gt:gt + 1]), waits=[(Sc, sq2), (V, kb.ctr[V])])
                vq1 = e(V, lambda v: v.tensor_max(gmax1[:, 0:1], gmax1[:, 0:1], stat[:, 7:8]))
            s2.close()

        for st in range(NST):
            _emit_st(st)
            if upto == "st0":
                raise _Cut()
        if upto == "passA":
            raise _Cut()

     sW.close()
     kb.region = "passB"
     with contextlib.ExitStack() as sB:
        def sb(n, s, d):
            return sB.enter_context(nc.sbuf_tensor(n, s, d))

        xnb = sb("b_xnb", [128, 2 * D], BF16)
        xnb_guard = [0, 0]
        sxb = sb("b_sxb", [128, 1], F32)
        sxv = sb("b_sxv", [128, 1], F32)
        shb = sb("b_shb", [128, 1], F32)
        shv = sb("b_shv", [128, 1], F32)
        scsc = sb("b_scsc", [1, 8], F32)
        prodB = sb("b_prodB", [128, NI * T], BF16)
        b1 = contextlib.ExitStack()
        xqd = b1.enter_context(nc.sbuf_tensor("b_xqd", [128, NJ * T], BF16))
        xqt = b1.enter_context(nc.sbuf_tensor("b_xqt", [128, D], BF16))
        wnat = b1.enter_context(nc.sbuf_tensor("b_wnat", [128, 2 * D], F32))
        wnat2 = b1.enter_context(nc.sbuf_tensor("b_wnat2", [128, 2 * D], F32))
        wqi = b1.enter_context(nc.sbuf_tensor("b_wqi", [128, 2 * D], BF16))
        wqi2 = b1.enter_context(nc.sbuf_tensor("b_wqi2", [128, 2 * D], BF16))
        wqt = b1.enter_context(nc.sbuf_tensor("b_wqt", [128, 2 * NJ * 128], BF16))
        wqt2 = b1.enter_context(nc.sbuf_tensor("b_wqt2", [128, 2 * NJ * 128], BF16))
        swsc = b1.enter_context(nc.sbuf_tensor("b_swsc", [128, NI], F32))
        swsc2 = b1.enter_context(nc.sbuf_tensor("b_swsc2", [128, NI], F32))

        tg1 = e(Tn, lambda t: t.transpose(P[6][0:1, 0:128], gmax1[:, 0:1], identf[:]),
                waits=bankw(6, [(V, kb.ctr[V])]))
        vg1 = e(V, lambda v: v.memset(ccb[:], 0.0), waits=[(Tn, tg1)])
        vg1 = e(V, lambda v: v.reduce_max(ccb[0:1, 0:1], P[6][0:1, 0:128], AX.X), waits=[(V, vg1)])
        bank_done(6, V)
        gcc = e(Gp, lambda g: g.dma_start(out=cc1_i[0:1, :], in_=ccb[0:1, :]),
                waits=[(V, vg1)])
        gcc = e(Gp, lambda g: g.collective_compute(
            "AllReduce", ALU.max, replica_groups=[[0, 1, 2, 3, 4, 5, 6, 7]],
            ins=[cc1_i[0:1, :]], outs=[cc1_o[0:1, :]]),
            waits=[(Gp, kb.ctr[Gp])], sem=Tn)
        GCC1 = kb.ctr[Tn]
        gcc = e(Gp, lambda g: g.dma_start(out=ccb[0:1, :], in_=cc1_o[0:1, :]),
                waits=[(Tn, GCC1)])
        GCC1d = kb.ctr[Gp]
        sg1 = e(Sc, lambda s: s.activation(scsc[0:1, 0:2], clipb[0:1, 0:2], AF.Sigmoid),
                waits=[(Gp, GCC1d), (V, kb.ctr[V])])
        vg1 = e(V, lambda v: v.tensor_mul(scsc[0:1, 2:3], scsc[0:1, 0:1], ccb[0:1, 0:1]),
                waits=[(Sc, sg1)])
        vg1 = e(V, lambda v: v.tensor_scalar_mul(scsc[0:1, 2:3], scsc[0:1, 2:3], 1.0 / 127.0))
        vg1 = e(V, lambda v: v.reciprocal(scsc[0:1, 3:4], scsc[0:1, 2:3]))
        tb1 = e(Tn, lambda t: t.matmul(P[6][:, 0:2], onesf[0:1, :], scsc[0:1, 2:4],
                                       start=True, stop=True), waits=bankw(6, [(V, vg1)]))
        vg1 = e(V, lambda v: v.tensor_copy(sxv[:, 0:1], P[6][:, 0:1]), waits=[(Tn, tb1)])
        vg1 = e(V, lambda v: v.tensor_copy(sxb[:, 0:1], P[6][:, 1:2]))
        bank_done(6, V)
        VSX = kb.ctr[V]
        if upto == "ar1":
            raise _Cut()
        for gt in range(NT):
            par = gt % 2
            xnd = e(Sy, lambda y, gt=gt, p=par: y.dma_start(
                out=xnb[:, p * D:(p + 1) * D], in_=xnew_h[gt * 128:(gt + 1) * 128, :]),
                waits=[(V, xnb_guard[par])] + (kb.now(V, Sc, Tn, Sy) if gt == 0 else []))
            vq = e(V, lambda v, gt=gt, p=par: v.tensor_scalar(
                scr1[:, 0:D], xnb[:, p * D:(p + 1) * D], m2s[:, gt:gt + 1],
                i2s[:, gt:gt + 1], ALU.add, ALU.mult),
                waits=[(V, VSX), (Sc, kb.ctr[Sc]), (Sy, xnd)])
            xnb_guard[par] = kb.ctr[V]
            vq = e(V, lambda v: v.tensor_scalar(scr1[:, 0:D], scr1[:, 0:D], sxb[:, 0:1],
                                                MAGIC, ALU.mult, ALU.add))
            vq = e(V, lambda v: v.tensor_scalar(scr1[:, 0:D], scr1[:, 0:D], -MAGIC, None, ALU.add))
            vq = e(V, lambda v: v.tensor_scalar(xqt[:, 0:D], scr1[:, 0:D], -128.0, 127.0,
                                                ALU.max, ALU.min))
            ttx = 0
            for j in range(NJ):
                ttx = e(Tn, lambda t, j=j: t.transpose(
                    PT[:, j * 128:(j + 1) * 128], xqt[:, j * 128:(j + 1) * 128], ident[:]),
                    waits=(bankw(7, [(V, vq)]) if j == 0 else ()))
            vq = e(V, lambda v, gt=gt: v.tensor_copy(
                xqd[:].rearrange("p (j t) -> p j t", j=NJ)[:, :, gt * 128:(gt + 1) * 128],
                PT[:, 0:1024].rearrange("p (j t) -> p j t", j=NJ)), waits=[(Tn, ttx)], nsw=True)
            bank_done(7, V)
        VXQ = kb.ctr[V]
        if upto == "xq":
            raise _Cut()

        gq = e(Gp, lambda g: g.dma_start(out=wnat[:, 0:D], in_=wg[0:128, :]),
               waits=[(V, kb.ctr[V]), (Sc, kb.ctr[Sc]), (Tn, kb.ctr[Tn])])
        gq = e(Gp, lambda g: g.dma_start(out=wnat2[:, 0:D], in_=wu[0:128, :]))
        gqm = {0: gq}
        wq_mm_done = [0, 0]
        prod_read = [0]
        for ic in range(NI):
            par = ic % 2
            if ic + 1 < NI:
                gq = e(Gp, lambda g, ic=ic: g.dma_start(
                    out=wnat[:, ((ic + 1) % 2) * D:((ic + 1) % 2) * D + D],
                    in_=wg[(ic + 1) * 128:(ic + 2) * 128, :]),
                    waits=[(V, kb.ctr[V])])
                gq = e(Gp, lambda g, ic=ic: g.dma_start(
                    out=wnat2[:, ((ic + 1) % 2) * D:((ic + 1) % 2) * D + D],
                    in_=wu[(ic + 1) * 128:(ic + 2) * 128, :]))
                gqm[ic + 1] = gq
            for (wn, wi, wt, ssc_, bank) in [(wnat, wqi, wqt, swsc, 0), (wnat2, wqi2, wqt2, swsc2, 1)]:
                vw = e(V, lambda v, ic=ic, p=par, wn=wn: v.reduce_max(
                    stat[:, 0:1], wn[:, p * D:(p + 1) * D], AX.X, apply_absolute_value=True),
                    waits=[(Gp, gqm[ic]), (V, kb.ctr[V]), (Tn, wq_mm_done[par])])
                vw = e(V, lambda v, ic=ic, ssc_=ssc_: v.tensor_scalar_mul(
                    ssc_[:, ic:ic + 1], stat[:, 0:1], 1.0 / 127.0))
                vw = e(V, lambda v: v.reciprocal(stat[:, 1:2], stat[:, 0:1]))
                vw = e(V, lambda v: v.tensor_scalar_mul(stat[:, 1:2], stat[:, 1:2], 127.0))
                vw = e(V, lambda v, p=par, wn=wn: v.tensor_scalar(
                    scr1[:, 0:D], wn[:, p * D:(p + 1) * D], stat[:, 1:2], MAGIC,
                    ALU.mult, ALU.add), waits=[(V, vw)])
                vw = e(V, lambda v: v.tensor_scalar(scr1[:, 0:D], scr1[:, 0:D], -MAGIC, None, ALU.add))
                vw = e(V, lambda v, p=par, wi=wi: v.tensor_scalar(
                    wi[:, p * D:(p + 1) * D], scr1[:, 0:D], -128.0, 127.0, ALU.max, ALU.min))
                ttw = 0
                for j in range(NJ):
                    ttw = e(Tn, lambda t, j=j, p=par, wi=wi: t.transpose(
                        PT[:, j * 128:(j + 1) * 128], wi[:, p * D + j * 128:p * D + j * 128 + 128],
                        ident[:]), waits=(bankw(7, [(V, vw)]) if j == 0 else ()))
                vw = e(V, lambda v, p=par, wt=wt: v.tensor_copy(
                    wt[:, p * 1024:(p + 1) * 1024], PT[:, 0:1024]), waits=[(Tn, ttw)], nsw=True)
                bank_done(7, V)
                vw = e(V, lambda v, ic=ic, ssc_=ssc_, b=bank: v.scalar_tensor_tensor(
                    stat[:, 2 + b:3 + b], sxv[:, 0:1], 1.0, ssc_[:, ic:ic + 1], ALU.mult, ALU.mult),
                    waits=[(V, vw), (Sc, kb.ctr[Sc])])
            VWQ = kb.ctr[V]
            for stq in range(T // 512):
                tg_ = 0
                for j in range(NJ):
                    tg_ = e(Tn, lambda t, j=j, p=par, stq=stq: t.matmul(
                        P[0][:, 0:512], wqt[:, p * 1024 + j * 128:p * 1024 + j * 128 + 128],
                        xqd[:, j * T + stq * 512:j * T + stq * 512 + 512],
                        start=(j == 0), stop=(j == NJ - 1)),
                        waits=(bankw(0, [(V, VWQ), (V, VXQ)]) if j == 0 else ()))
                tu_ = 0
                for j in range(NJ):
                    tu_ = e(Tn, lambda t, j=j, p=par, stq=stq: t.matmul(
                        P[1][:, 0:512], wqt2[:, p * 1024 + j * 128:p * 1024 + j * 128 + 128],
                        xqd[:, j * T + stq * 512:j * T + stq * 512 + 512],
                        start=(j == 0), stop=(j == NJ - 1)),
                        waits=(bankw(1) if j == 0 else ()))
                sgt = e(Sc, lambda s: s.activation(scr1[:, 0:512], P[0][:, 0:512], AF.Silu,
                                                   scale=stat[:, 2:3]),
                        waits=[(Tn, tg_), (V, VWQ), (V, prod_read[0])])
                bank_done(0, Sc)
                vgt = e(V, lambda v: v.tensor_scalar(scr2[:, 0:512], P[1][:, 0:512], stat[:, 3:4],
                                                     None, ALU.mult), waits=[(Tn, tu_), (Sc, sgt)])
                bank_done(1, V)
                vgt = e(V, lambda v, ic=ic, stq=stq: v.tensor_mul(
                    prodB[:, ic * T + stq * 512:ic * T + stq * 512 + 512], scr1[:, 0:512],
                    scr2[:, 0:512]), nsw=True)
                prod_read[0] = vgt
                vgt = e(V, lambda v, ic=ic, stq=stq: v.reduce_max(
                    stat[:, 4:5], prodB[:, ic * T + stq * 512:ic * T + stq * 512 + 512], AX.X,
                    apply_absolute_value=True))
                vgt = e(V, lambda v, ic=ic: v.tensor_max(gmax2[:, ic:ic + 1], gmax2[:, ic:ic + 1],
                                                         stat[:, 4:5]))
                wq_mm_done[par] = tu_
        if upto == "gateup":
            raise _Cut()
        dump("xqd", lambda: xqd[:], [128, NJ * T], BF16)
        dump("prodB", lambda: prodB[:], [128, NI * T], BF16)
        b1.close()
        vg2 = e(V, lambda v: v.reduce_max(stat[:, 0:1], gmax2[:, 0:NI], AX.X),
                waits=[(V, kb.ctr[V])])
        tg2 = e(Tn, lambda t: t.transpose(P[6][0:1, 0:128], stat[:, 0:1], identf[:]),
                waits=bankw(6, [(V, kb.ctr[V])]))
        vg2 = e(V, lambda v: v.memset(ccb[:], 0.0), waits=[(Tn, tg2)])
        vg2 = e(V, lambda v: v.reduce_max(ccb[0:1, 0:1], P[6][0:1, 0:128], AX.X), waits=[(V, vg2)])
        bank_done(6, V)
        gc2 = e(Gp, lambda g: g.dma_start(out=cc2_i[0:1, :], in_=ccb[0:1, :]), waits=[(V, vg2)])
        gc2 = e(Gp, lambda g: g.collective_compute(
            "AllReduce", ALU.max, replica_groups=[[0, 1, 2, 3, 4, 5, 6, 7]],
            ins=[cc2_i[0:1, :]], outs=[cc2_o[0:1, :]]),
            waits=[(Gp, kb.ctr[Gp])], sem=Tn)
        GCC2 = kb.ctr[Tn]
        gc2 = e(Gp, lambda g: g.dma_start(out=ccb[0:1, :], in_=cc2_o[0:1, :]), waits=[(Tn, GCC2)])
        GCC2d = kb.ctr[Gp]
        vg2 = e(V, lambda v: v.tensor_mul(scsc[0:1, 4:5], scsc[0:1, 1:2], ccb[0:1, 0:1]),
                waits=[(Gp, GCC2d), (V, kb.ctr[V])])
        vg2 = e(V, lambda v: v.tensor_scalar_mul(scsc[0:1, 4:5], scsc[0:1, 4:5], 1.0 / 127.0))
        vg2 = e(V, lambda v: v.reciprocal(scsc[0:1, 5:6], scsc[0:1, 4:5]))
        tb2 = e(Tn, lambda t: t.matmul(P[6][:, 0:2], onesf[0:1, :], scsc[0:1, 4:6],
                                       start=True, stop=True), waits=bankw(6, [(V, vg2)]))
        vg2 = e(V, lambda v: v.tensor_copy(shv[:, 0:1], P[6][:, 0:1]), waits=[(Tn, tb2)])
        vg2 = e(V, lambda v: v.tensor_copy(shb[:, 0:1], P[6][:, 1:2]))
        bank_done(6, V)
        VSH = kb.ctr[V]
        if upto == "ar2":
            raise _Cut()
        for ic in range(NI):
            for q4 in range(T // 512):
                off = ic * T + q4 * 512
                vq = e(V, lambda v, off=off: v.tensor_scalar(
                    scr1[:, 0:512], prodB[:, off:off + 512], shb[:, 0:1], MAGIC,
                    ALU.mult, ALU.add), waits=[(V, VSH)] if (ic == 0 and q4 == 0) else ())
                vq = e(V, lambda v: v.tensor_scalar(scr1[:, 0:512], scr1[:, 0:512], -MAGIC,
                                                    None, ALU.add))
                vq = e(V, lambda v, off=off: v.tensor_scalar(
                    prodB[:, off:off + 512], scr1[:, 0:512], -128.0, 127.0, ALU.max, ALU.min))
        VH2 = kb.ctr[V]
        if upto == "prodq":
            raise _Cut()
        wdT = sb("b_wdT", [128, NI * D], BF16)
        sdb = sb("b_sdb", [128, D], F32)
        sdr = sb("b_sdr", [8, 128], F32)
        sdrow = sb("b_sdrow", [1, D], F32)
        wdn = sb("b_wdn", [128, II], F32)
        wdq = sb("b_wdq", [128, II], BF16)
        swd = sb("b_swd", [128, NJ], F32)
        gwd_mark = 0
        twd_done = 0
        for jh in range(NJ):
            gwd = e(Gp, lambda g, jh=jh: g.dma_start(out=wdn[:, 0:II],
                                                     in_=wd[jh * 128:(jh + 1) * 128, :]),
                    waits=[(V, kb.ctr[V]), (Tn, twd_done)])
            vw = e(V, lambda v, jh=jh: v.reduce_max(stat[:, 0:1], wdn[:, 0:II], AX.X,
                                                    apply_absolute_value=True),
                   waits=[(Gp, kb.ctr[Gp])])
            vw = e(V, lambda v, jh=jh: v.tensor_scalar_mul(swd[:, jh:jh + 1], stat[:, 0:1],
                                                           1.0 / 127.0))
            vw = e(V, lambda v: v.reciprocal(stat[:, 1:2], stat[:, 0:1]))
            vw = e(V, lambda v: v.tensor_scalar_mul(stat[:, 1:2], stat[:, 1:2], 127.0))
            for q4 in range(3):
                pass
            vw2 = vw
            for piece in range(4):
                off = piece * 704
                vw2 = e(V, lambda v, off=off: v.tensor_scalar(
                    scr1[:, 0:704], wdn[:, off:off + 704], stat[:, 1:2], MAGIC, ALU.mult, ALU.add),
                    waits=[(V, vw)] if piece == 0 else ())
                vw2 = e(V, lambda v: v.tensor_scalar(scr1[:, 0:704], scr1[:, 0:704], -MAGIC,
                                                     None, ALU.add))
                vw2 = e(V, lambda v, off=off: v.tensor_scalar(
                    wdq[:, off:off + 704], scr1[:, 0:704], -128.0, 127.0, ALU.max, ALU.min))
            ttd = 0
            for bi in range(NI):
                bw = bankw(7, [(V, vw2)]) if bi == 0 else (
                    [(V, kb.ctr[V])] if bi % 8 == 0 else [])
                ttd = e(Tn, lambda t, bi=bi: t.transpose(
                    PT[:, (bi % 8) * 128:(bi % 8) * 128 + 128], wdq[:, bi * 128:(bi + 1) * 128],
                    ident[:]), waits=bw)
                if bi % 8 == 7 or bi == NI - 1:
                    lo = (bi // 8) * 8
                    vw2 = e(V, lambda v, lo=lo, bi=bi, jh=jh: v.tensor_copy(
                        wdT[:].rearrange("p (i h) -> p i h", i=NI)[:, lo:bi + 1, jh * 128:(jh + 1) * 128],
                        PT[:, 0:(bi + 1 - lo) * 128].rearrange("p (i h) -> p i h", h=128)),
                        waits=[(Tn, ttd)], nsw=True)
                    bank_done(7, V)
            twd_done = kb.ctr[Tn]
        tsd = e(Tn, lambda t: t.transpose(P[6][0:8, 0:128], swd[:, 0:NJ], identf[:]),
                waits=bankw(6, [(V, kb.ctr[V])]))
        vsd = e(V, lambda v: v.tensor_copy(sdr[0:8, 0:128], P[6][0:8, 0:128]), waits=[(Tn, tsd)])
        bank_done(6, V)
        gsd = e(Gp, lambda g: g.dma_start(out=sdrow[0:1, :].rearrange("a (j p) -> a j p", j=NJ),
                                          in_=sdr[0:8, 0:128].rearrange("j p -> 1 j p") if False else sdr[0:8, 0:128]),
                waits=[(V, vsd)])
        GSD = kb.ctr[Gp]
        for hf in range(2):
            tsd = e(Tn, lambda t, h=hf: t.matmul(P[6][:, 0:512], onesf[0:1, :],
                                                 sdrow[0:1, h * 512:h * 512 + 512],
                                                 start=True, stop=True),
                    waits=bankw(6, [(Gp, GSD)]))
            vsd = e(V, lambda v, h=hf: v.tensor_scalar(
                sdb[:, h * 512:h * 512 + 512], P[6][:, 0:512], shv[:, 0:1], None, ALU.mult),
                waits=[(Tn, tsd)])
            bank_done(6, V)
        VSD = kb.ctr[V]
        if upto == "wd":
            raise _Cut()
        for gt in range(NT):
            xnd = e(Sy, lambda y, gt=gt, p=(gt % 2): y.dma_start(
                out=xnb[:, p * D:(p + 1) * D], in_=xnew_h[gt * 128:(gt + 1) * 128, :]),
                waits=[(V, xnb_guard[gt % 2])])
            for hf in range(2):
                td = 0
                for bi in range(NI):
                    td = e(Tn, lambda t, bi=bi, gt=gt, h=hf: t.matmul(
                        P[3][:, 0:512], prodB[:, bi * T + gt * 128:bi * T + gt * 128 + 128],
                        wdT[:, bi * D + h * 512:bi * D + h * 512 + 512],
                        start=(bi == 0), stop=(bi == NI - 1)),
                        waits=(bankw(3, [(V, VSD), (V, VH2)]) if bi == 0 else ()))
                vd = e(V, lambda v, h=hf: v.tensor_mul(scr1[:, h * 512:h * 512 + 512],
                                                       P[3][:, 0:512], sdb[:, h * 512:h * 512 + 512]),
                       waits=[(Tn, td)] + ([("out", gt * 16)] if hf == 0 else []), nsw=True)
                bank_done(3, V)
                vd = e(V, lambda v, gt=gt, h=hf, p=(gt % 2): v.tensor_add(
                    scr1[:, h * 512:h * 512 + 512], scr1[:, h * 512:h * 512 + 512],
                    xnb[:, p * D + h * 512:p * D + h * 512 + 512]),
                    waits=[(Sy, xnd)] if hf == 0 else [], nsw=True)
            xnb_guard[gt % 2] = kb.ctr[V]
            e(Sy, lambda y, gt=gt: y.dma_start(out=out_h[gt * 128:(gt + 1) * 128, :],
                                               in_=scr1[:, 0:D]),
              waits=[(V, kb.ctr[V])], sem="out")

    except _Cut:
        pass
    kb.wait(Sy, [("out", kb.ctr["out"])])

    with nc.Block() as block:
        @block.sync
        def _(y):
            for r in kb.sched[Sy]:
                r(y)

        @block.vector
        def _(v):
            for r in kb.sched[V]:
                r(v)

        @block.scalar
        def _(s):
            for r in kb.sched[Sc]:
                r(s)

        @block.tensor
        def _(t):
            for r in kb.sched[Tn]:
                r(t)

        @block.gpsimd
        def _(g):
            for r in kb.sched[Gp]:
                r(g)
    if upto is None:
        kb.es.close()
    return nc


_CONSTS = None


def _consts():
    global _CONSTS
    if _CONSTS is None:
        _CONSTS = {
            "c_ident": np.eye(128, dtype=ml_dtypes.bfloat16),
            "c_masku": np.triu(np.ones((128, 128), np.uint8), 1),
            "c_maskui": np.triu(np.ones((128, 128), np.uint8), 0),
            "c_maskl": np.tril(np.ones((128, 128), np.uint8), -1),
            "c_ones": np.ones((1, 128), dtype=ml_dtypes.bfloat16),
            "c_onesf": np.ones((1, 128), dtype=np.float32),
            "c_identf": np.eye(128, dtype=np.float32),
        }
        E = np.zeros((16, NJ * 128), dtype=ml_dtypes.bfloat16)
        F = np.zeros((128, NJ * 16), dtype=ml_dtypes.bfloat16)
        for j in range(NJ):
            for m in range(128):
                h = 2 * j + (1 if m >= 64 else 0)
                E[h, j * 128 + m] = 1
                F[m, j * 16 + h] = 1
        _CONSTS["c_E"] = E
        _CONSTS["c_F"] = F
    return _CONSTS


_NC_CACHE = {}


def kernel(**inputs):
    T_ = inputs["L0_attn_x"].shape[1]
    if T_ not in _NC_CACHE:
        _NC_CACHE[T_] = build(T_, serial=SELF_WAIT_V)
    nc = _NC_CACHE[T_]
    consts = _consts()
    x = np.ascontiguousarray(inputs["L0_attn_x"])
    B = x.shape[0]
    base = {
        "q_w": inputs["L0_attn_q_w"], "k_w": inputs["L0_attn_k_w"],
        "v_w": inputs["L0_attn_v_w"], "alpha_w": inputs["L0_attn_alpha_w"],
        "alpha_b": inputs["L0_attn_alpha_b"], "beta_w": inputs["L0_attn_beta_w"],
        "beta_b": inputs["L0_attn_beta_b"], "out_w": inputs["L0_attn_out_w"],
        "conv_w": np.ascontiguousarray(inputs["L0_attn_conv_w"].reshape(D, 4)),
        "conv_b": inputs["L0_attn_conv_b"], "norm_in_w": inputs["L0_attn_norm_in_w"],
        "head_norm_w": np.ascontiguousarray(inputs["L0_attn_head_norm_w"].reshape(H * 64)),
        "u1_w": inputs["L0_attn_u1_w"], "u1_b": inputs["L0_attn_u1_b"],
        "u2_w": inputs["L0_attn_u2_w"], "u2_b": inputs["L0_attn_u2_b"],
        "ff_norm_w": inputs["L0_attn_ff_norm_w"], "ff_w1": inputs["L0_attn_ff_w1"],
        "ff_w3": inputs["L0_attn_ff_w3"], "ff_w2": inputs["L0_attn_ff_w2"],
        "clip_x": inputs["L0_ffn_clip_x"], "clip_down": inputs["L0_ffn_clip_down"],
        "w_gate": inputs["L0_ffn_w_gate"], "w_up": inputs["L0_ffn_w_up"],
        "w_down": inputs["L0_ffn_w_down"],
    }
    base.update(consts)
    in_maps = []
    for b in range(B):
        m = dict(base)
        m["x"] = x[b]
        in_maps.append(m)
    res = run_bass_kernel_spmd(nc, in_maps, core_ids=list(range(B)))
    out = np.stack([res.results[b]["out"] for b in range(B)], axis=0)
    return out.astype(np.float32)
